# version 19
# baseline (speedup 1.0000x reference)
import numpy as np
import ml_dtypes
from contextlib import ExitStack

import concourse.bass as bass
import concourse.mybir as mybir
from concourse.bass_utils import run_bass_kernel_spmd

F32 = mybir.dt.float32
BF16 = mybir.dt.bfloat16
AF = mybir.ActivationFunctionType
ALU = mybir.AluOpType
AX = mybir.AxisListType

BF = ml_dtypes.bfloat16


class CFG:
    def __init__(self, T=2048, D=1024, H=4096, E=8, NH=8, DQK=64, TB=256, JCG=8):
        self.T, self.D, self.H, self.E, self.NH, self.DQK, self.TB, self.JCG = (
            T, D, H, E, NH, DQK, TB, JCG)
        self.DV = D // NH
        self.TT = T // 128
        self.DC = D // 128
        self.JC = H // 128
        self.NG = self.JC // JCG
        self.TBN = T // TB
        self.NTB = T // 512
        self.MAXPOS = 10
        assert TB % 128 == 0 and self.TB // 128 == 2


ENGINES = ("sync", "scalar", "vector", "tensor", "gpsimd")


class Ctl:

    def __init__(self, nc, ctx):
        self.nc = nc
        self.q = {e: [] for e in ENGINES}
        self.sems = {}
        self.ticks = {}
        self.waited = {e: {} for e in ENGINES}
        self.writer = {}
        self.readers = {}
        self._ctx = ctx
        for name in ("t_sync", "t_scalar", "t_vector", "t_tensor", "t_gpsimd"):
            self._add_sem(name)

    def _add_sem(self, name):
        if name not in self.sems:
            h = self._ctx.enter_context(self.nc.semaphore(name))
            self.sems[name] = h
            self.ticks[name] = 0

    def _emit_wait(self, eng, semname, val):
        if val <= 0:
            return
        if self.waited[eng].get(semname, 0) >= val:
            return
        assert val <= self.ticks[semname], (
            f"{eng} waits {semname}>={val}, only {self.ticks[semname]} produced")
        self.waited[eng][semname] = val
        h = self.sems[semname]
        self.q[eng].append(lambda e, h=h, v=val: e.wait_ge(h, v))

    def _dep_waits(self, eng, reads, writes):
        for b in tuple(reads) + tuple(writes):
            w = self.writer.get(b)
            if w:
                self._emit_wait(eng, w[0], w[1])
        for b in writes:
            for sname, v in self.readers.get(b, {}).items():
                self._emit_wait(eng, sname, v)

    def _record(self, sem, val, reads, writes):
        for b in writes:
            self.writer[b] = (sem, val)
            self.readers[b] = {}
        for b in reads:
            self.readers.setdefault(b, {})[sem] = val

    def op(self, eng, fn, reads=(), writes=()):
        self._dep_waits(eng, reads, writes)
        sem = "t_" + eng
        self.ticks[sem] += 1
        val = self.ticks[sem]
        h = self.sems[sem]
        self.q[eng].append(lambda e, fn=fn, h=h: fn(e).then_inc(h, 1))
        self._record(sem, val, reads, writes)
        return sem, val

    def dma(self, eng, sem, fns, reads=(), writes=()):
        self._add_sem(sem)
        self._dep_waits(eng, reads, writes)
        h = self.sems[sem]
        for fn in fns:
            self.ticks[sem] += 16
            self.q[eng].append(lambda e, fn=fn, h=h: fn(e).then_inc(h, 16))
        self._record(sem, self.ticks[sem], reads, writes)
        return sem, self.ticks[sem]

    def group(self, eng, fns, reads=(), writes=()):
        self._dep_waits(eng, reads, writes)
        sem = "t_" + eng
        self.ticks[sem] += 1
        val = self.ticks[sem]
        h = self.sems[sem]
        for fn in fns[:-1]:
            self.q[eng].append(fn)
        self.q[eng].append(lambda e, fn=fns[-1], h=h: fn(e).then_inc(h, 1))
        self._record(sem, val, reads, writes)
        return sem, val

    def barrier(self):
        snap = {k: v for k, v in self.ticks.items() if v > 0}
        for e in ENGINES:
            for sname, v in snap.items():
                self._emit_wait(e, sname, v)

    def flush(self):
        qs = self.q
        self.q = {e: [] for e in ENGINES}
        with self.nc.Block() as block:
            @block.sync
            def _(e):
                for f in qs["sync"]:
                    f(e)

            @block.scalar
            def _(e):
                for f in qs["scalar"]:
                    f(e)

            @block.vector
            def _(e):
                for f in qs["vector"]:
                    f(e)

            @block.tensor
            def _(e):
                for f in qs["tensor"]:
                    f(e)

            @block.gpsimd
            def _(e):
                for f in qs["gpsimd"]:
                    f(e)


def build_program(cfg: CFG, dbg=False):
    c = cfg
    nc = bass.Bass("TRN2", target_bir_lowering=False)

    x_in = nc.dram_tensor("x_in", [c.T, c.D], F32, kind="ExternalInput").ap()
    y_out = nc.dram_tensor("y_out", [c.T, c.D], F32, kind="ExternalOutput").ap()
    w1s = nc.dram_tensor("w1s", [c.E, c.NG, 128, c.DC * c.JCG * 128], F32,
                         kind="ExternalInput").ap()
    w2s = nc.dram_tensor("w2s", [c.E, c.NG, 128, c.JCG * c.D], F32,
                         kind="ExternalInput").ap()
    wqks = nc.dram_tensor("wqks", [128, c.DC * c.D], F32, kind="ExternalInput").ap()
    wvs = nc.dram_tensor("wvs", [128, c.DC * c.D], F32, kind="ExternalInput").ap()
    wos = nc.dram_tensor("wos", [128, c.DC * c.D], F32, kind="ExternalInput").ap()
    wrs = nc.dram_tensor("wrs", [128, c.DC * c.E], F32, kind="ExternalInput").ap()
    b_out_t = nc.dram_tensor("b_out_t", [c.D], F32, kind="ExternalInput").ap()
    STRW = c.T + 384
    SFULL = c.T + 1920
    strips = nc.dram_tensor("strips", [c.NH, 128, SFULL], F32,
                            kind="ExternalInput").ap()
    if dbg:
        dbg_x1 = nc.dram_tensor("dbg_x1", [c.T, c.D], F32, kind="ExternalOutput").ap()
        dbg_x2 = nc.dram_tensor("dbg_x2", [c.T, c.D], F32, kind="ExternalOutput").ap()
        dbg_chc = nc.dram_tensor("dbg_chc", [128, c.NH], F32, kind="ExternalOutput").ap()
    ident_d = nc.inline_tensor(np.eye(128, dtype=BF), name="ident_bf").ap()
    onesf_d = nc.inline_tensor(np.ones((1, 128), dtype=BF), name="ones_b").ap()

    JGW = c.JCG * 128

    ctx = ExitStack()
    with ctx:
        ctl = Ctl(nc, ctx)
        sb = lambda *a: ctx.enter_context(nc.sbuf_tensor(*a))
        psum = lambda *a: ctx.enter_context(nc.psum_tensor(*a))

        x_sb = sb("x_sb", [128, c.TT, c.D], F32)
        idn = sb("idn", [128, 128], BF16)
        negb = sb("negb", [128, 1], BF16)
        onesf = sb("onesf", [1, 128], BF16)
        chC = sb("chC", [128, c.NH], F32)
        sc1t = sb("sc1t", [1, 1], BF16)
        rtr = sb("rtr", [128, c.DC, c.E], BF16)
        epsc = sb("epsc", [128, 1], F32)
        st_s = sb("st_s", [128, c.TT], F32)
        st_nm = sb("st_nm", [128, c.TT], F32)
        st_qq = sb("st_qq", [128, c.TT], F32)
        st_m2 = sb("st_m2", [128, c.TT], F32)
        st_var = sb("st_var", [128, c.TT], F32)
        st_std = sb("st_std", [128, c.TT], F32)
        st_rs = sb("st_rs", [128, c.TT], F32)
        g_e = sb("g_e", [128, 2, c.E], F32)
        g_z = sb("g_z", [128, 2], F32)
        g_m1 = sb("g_m1", [128, 2], F32)
        g_nm = sb("g_nm", [128, 2], F32)
        g_k = sb("g_k", [128, 2, c.E], F32)
        g_m2 = sb("g_m2", [128, 2], F32)
        g_rz = sb("g_rz", [128, 2], F32)

        ps_a = psum("ps_a", [128, 2048], F32)
        ps_c = psum("ps_c", [128, 512], F32)
        ps_d = psum("ps_d", [128, 512], F32)
        ps_f0 = psum("ps_f0", [128, 512], F32)
        ps_f1 = psum("ps_f1", [128, 512], F32)

        x3 = x_in.rearrange("(t p) d -> t p d", p=128)
        nb = max(1, c.TT // 4)
        for bi in range(0, c.TT, nb):
            eng = ("sync", "scalar")[(bi // nb) % 2]
            ctl.dma(eng, f"d_x{bi}",
                    [(lambda e, t=t: e.dma_start(out=x_sb[:, t, :], in_=x3[t]))
                     for t in range(bi, min(c.TT, bi + nb))],
                    writes=tuple(f"x{t}" for t in range(bi, min(c.TT, bi + nb))))
        ctl.dma("gpsimd", "d_ini",
                [lambda e: e.dma_start(out=idn[:], in_=ident_d),
                 lambda e: e.dma_start(out=onesf[:], in_=onesf_d),
                 lambda e: e.dma_start(
                     out=rtr[:].rearrange("p dc k -> p (dc k)"), in_=wrs)],
                writes=("idn", "onesf", "rtr"))
        ctl.op("gpsimd", lambda e: e.memset(epsc[:], 1e-5), writes=("eps",))

        def ln_stats(sq, sqkey):
            for t in range(c.TT):
                ctl.op("scalar",
                       lambda e, t=t: e.activation(sq, x_sb[:, t, :], AF.Square,
                                                   accum_out=st_qq[:, t:t + 1]),
                       reads=(f"x{t}",), writes=(sqkey, f"qq{t}"))
                ctl.op("vector",
                       lambda e, t=t: e.reduce_sum(st_s[:, t:t + 1], x_sb[:, t, :],
                                                   axis=AX.X),
                       reads=(f"x{t}",), writes=(f"s{t}",))
                ctl.op("vector",
                       lambda e, t=t: e.tensor_scalar(
                           st_nm[:, t:t + 1], st_s[:, t:t + 1], -1.0 / c.D, None,
                           ALU.mult),
                       reads=(f"s{t}",), writes=(f"nm{t}",))
                ctl.op("vector",
                       lambda e, t=t: e.tensor_mul(st_m2[:, t:t + 1],
                                                   st_nm[:, t:t + 1],
                                                   st_nm[:, t:t + 1]),
                       reads=(f"nm{t}",), writes=(f"m2{t}",))
                ctl.op("vector",
                       lambda e, t=t: e.scalar_tensor_tensor(
                           st_var[:, t:t + 1], st_qq[:, t:t + 1], 1.0 / c.D,
                           st_m2[:, t:t + 1], ALU.mult, ALU.subtract),
                       reads=(f"qq{t}", f"m2{t}"), writes=(f"var{t}",))
            for t in range(c.TT):
                ctl.op("scalar",
                       lambda e, t=t: e.activation(st_std[:, t:t + 1],
                                                   st_var[:, t:t + 1], AF.Sqrt,
                                                   bias=epsc[:]),
                       reads=(f"var{t}", "eps"), writes=(f"std{t}",))
                ctl.op("vector",
                       lambda e, t=t: e.reciprocal(st_rs[:, t:t + 1],
                                                   st_std[:, t:t + 1]),
                       reads=(f"std{t}",), writes=(f"rs{t}",))
        def ln_and_fm(ln_fm, lnkey, lntm):
            for t in range(c.TT):
                r = t % 2
                ctl.op("vector",
                       lambda e, t=t, r=r: e.tensor_scalar(
                           lntm[:, r, :], x_sb[:, t, :], st_nm[:, t:t + 1],
                           st_rs[:, t:t + 1], ALU.add, ALU.mult),
                       reads=(f"x{t}", f"nm{t}", f"rs{t}"), writes=(f"lntm{r}",))
                for hb, pf in ((0, ps_f0), (1, ps_f1)):
                    ctl.group("tensor",
                              [(lambda e, r=r, dc=dc, hb=hb, pf=pf: e.matmul(
                                  pf[:, (dc - hb * 4) * 128:(dc - hb * 4 + 1) * 128],
                                  lntm[:, r, dc * 128:(dc + 1) * 128], idn[:],
                                  start=True, stop=True))
                               for dc in range(hb * 4, hb * 4 + 4)],
                              reads=(f"lntm{r}", "idn"), writes=(f"pf{hb}",))
                    ctl.op("scalar",
                           lambda e, t=t, hb=hb, pf=pf: e.copy(
                               ln_fm[:, hb * 4:hb * 4 + 4, t * 128:(t + 1) * 128],
                               pf[:].rearrange("p (dc k) -> p dc k", k=128)),
                           reads=(f"pf{hb}",), writes=(f"{lnkey}{t}",))

        def router(ln_fm, lnkey, gates):
            for t in range(c.TT):
                r = t % 2
                psl = (ps_d, ps_c)[r]
                plk = ("pD", "pC")[r]
                ctl.group("tensor",
                          [(lambda e, t=t, dc=dc, psl=psl: e.matmul(
                              psl[:, 0:c.E],
                              ln_fm[:, dc, t * 128:(t + 1) * 128],
                              rtr[:, dc, :],
                              start=(dc == 0), stop=(dc == c.DC - 1)))
                           for dc in range(c.DC)],
                          reads=(f"{lnkey}{t}", "rtr"), writes=(plk,))
                ctl.op("vector",
                       lambda e, r=r, psl=psl: e.reduce_max(
                           g_nm[:, r:r + 1], psl[:, 0:c.E],
                           axis=AX.X, negate=True),
                       reads=(plk,), writes=(f"gnm{r}",))
                ctl.op("scalar",
                       lambda e, r=r, psl=psl: e.activation(
                           g_e[:, r, :], psl[:, 0:c.E],
                           AF.Exp, bias=g_nm[:, r:r + 1],
                           accum_out=g_z[:, r:r + 1]),
                       reads=(plk, f"gnm{r}"), writes=(f"ge{r}", f"gz{r}"))
                ctl.op("vector",
                       lambda e, r=r: e.reduce_max(g_m1[:, r:r + 1], g_e[:, r, :],
                                                   axis=AX.X),
                       reads=(f"ge{r}",), writes=(f"gm1{r}",))
                ctl.op("vector",
                       lambda e, r=r: e.tensor_scalar(g_k[:, r, :], g_e[:, r, :],
                                                      g_m1[:, r:r + 1], None,
                                                      ALU.is_ge),
                       reads=(f"ge{r}", f"gm1{r}"), writes=(f"gk{r}",))
                ctl.op("vector",
                       lambda e, r=r: e.tensor_sub(g_k[:, r, :], g_e[:, r, :],
                                                   g_k[:, r, :]),
                       reads=(f"ge{r}", f"gk{r}"), writes=(f"gk{r}",))
                ctl.op("vector",
                       lambda e, r=r: e.reduce_max(g_m2[:, r:r + 1], g_k[:, r, :],
                                                   axis=AX.X),
                       reads=(f"gk{r}",), writes=(f"gm2{r}",))
                ctl.op("vector",
                       lambda e, r=r: e.tensor_scalar(g_k[:, r, :], g_e[:, r, :],
                                                      g_m2[:, r:r + 1], None,
                                                      ALU.is_ge),
                       reads=(f"ge{r}", f"gm2{r}"), writes=(f"gk{r}",))
                ctl.op("vector",
                       lambda e, r=r: e.tensor_mul(g_k[:, r, :], g_e[:, r, :],
                                                   g_k[:, r, :]),
                       reads=(f"ge{r}", f"gk{r}"), writes=(f"gk{r}",))
                ctl.op("vector",
                       lambda e, r=r: e.reciprocal(g_rz[:, r:r + 1],
                                                   g_z[:, r:r + 1]),
                       reads=(f"gz{r}",), writes=(f"grz{r}",))
                ctl.op("vector",
                       lambda e, t=t, r=r: e.tensor_scalar(
                           gates[:, t, :], g_k[:, r, :], g_rz[:, r:r + 1], 0.5,
                           ALU.mult, ALU.mult),
                       reads=(f"gk{r}", f"grz{r}"), writes=(f"gates{t}",))

        def ffn(call):
            fctx = ExitStack()
            with fctx:
                fsb = lambda *a: fctx.enter_context(nc.sbuf_tensor(*a))
                ln_fm = fsb(f"ln_fm{call}", [128, c.DC, c.T], BF16)
                gates = fsb(f"gates{call}", [128, c.TT, c.E], F32)
                sq = fsb(f"sq{call}", [128, c.D], F32)
                lntm = fsb(f"lntm{call}", [128, 2, c.D], BF16)
                if call == 0:
                    ln_stats(sq[:], f"sq{call}")
                w1b = fsb(f"w1b{call}", [128, 2, c.DC, JGW], BF16)
                w2b = fsb(f"w2b{call}", [128, 2, c.JCG, c.D], BF16)
                h_sb = fsb(f"h_sb{call}", [128, 3, c.TB], BF16)

                LK = f"lf{call}_"
                n_gran = c.E * c.NG

                def issue_gran(idx):
                    e_, g_ = divmod(idx, c.NG)
                    par = idx % 2
                    ctl.dma("gpsimd", f"d_w{par}",
                            [(lambda en, e_=e_, g_=g_, par=par: en.dma_start(
                                out=w1b[:, par, :, :].rearrange(
                                    "p dc j -> p (dc j)"),
                                in_=w1s[e_, g_])),
                             (lambda en, e_=e_, g_=g_, par=par: en.dma_start(
                                out=w2b[:, par, :, :].rearrange(
                                    "p jc i -> p (jc i)"),
                                in_=w2s[e_, g_]))],
                            writes=(f"w1b{par}", f"w2b{par}"))

                issue_gran(0)
                if n_gran > 1:
                    issue_gran(1)
                ln_and_fm(ln_fm, LK, lntm)
                router(ln_fm, LK, gates)

                for idx in range(n_gran):
                    e_, g_ = divmod(idx, c.NG)
                    par = idx % 2
                    for tb in range(c.TBN):
                        t0 = tb * c.TB
                        tiles = (tb * 2, tb * 2 + 1)

                        def h_group(jj, ph, par=par, t0=t0, tiles=tiles):
                            psh = (ps_c, ps_d, ps_f0, ps_f1)[jj % 4]
                            ctl.group("tensor",
                                      [(lambda e, dc=dc, jj=jj, psh=psh, par=par,
                                        t0=t0: e.matmul(
                                          psh[:, 0:c.TB],
                                          w1b[:, par, dc, jj * 128:(jj + 1) * 128],
                                          ln_fm[:, dc, t0:t0 + c.TB],
                                          start=(dc == 0), stop=(dc == c.DC - 1)))
                                       for dc in range(c.DC)],
                                      reads=(f"w1b{par}",) + tuple(
                                          f"{LK}{t}" for t in tiles),
                                      writes=(("pC", "pD", "pf0", "pf1")[jj % 4],))

                        def gelu(jj, ph):
                            psh = (ps_c, ps_d, ps_f0, ps_f1)[jj % 4]
                            hs = jj % 3
                            ctl.op("scalar",
                                   lambda e, hs=hs, psh=psh: e.activation(
                                       h_sb[:, hs, :], psh[:, 0:c.TB], AF.Gelu),
                                   reads=(("pC", "pD", "pf0", "pf1")[jj % 4],),
                                   writes=(f"hsb{hs}",))

                        def y_group(jj, ph, par=par):
                            hs = jj % 3

                            def mk(tt, ic, b, jj=jj, hs=hs, par=par):
                                return lambda e: e.matmul(
                                    ps_a[:, b * 512:(b + 1) * 512],
                                    h_sb[:, hs, tt * 128:(tt + 1) * 128],
                                    w2b[:, par, jj, ic * 512:(ic + 1) * 512],
                                    start=(jj == 0), stop=(jj == c.JCG - 1))
                            if jj == 0:
                                for tt in range(2):
                                    for ic in range(2):
                                        b = 2 * tt + ic
                                        ctl.group("tensor", [mk(tt, ic, b)],
                                                  reads=(f"hsb{hs}",
                                                         f"w2b{par}"),
                                                  writes=(f"pA{b}",))
                            else:
                                fns = [mk(tt, ic, 2 * tt + ic)
                                       for tt in range(2) for ic in range(2)]
                                ctl.group("tensor", fns,
                                          reads=(f"hsb{hs}", f"w2b{par}"),
                                          writes=("pA0", "pA1", "pA2", "pA3"))

                        h_group(0, 0)
                        gelu(0, 0)
                        for jj in range(1, c.JCG):
                            ph = jj % 2
                            h_group(jj, ph)
                            gelu(jj, ph)
                            y_group(jj - 1, (jj - 1) % 2)
                        y_group(c.JCG - 1, (c.JCG - 1) % 2)

                        for tt in range(2):
                            t = tiles[tt]
                            for ic in range(2):
                                b = 2 * tt + ic
                                ctl.op("vector",
                                       lambda e, t=t, ic=ic, b=b, e_=e_:
                                       e.scalar_tensor_tensor(
                                           x_sb[:, t, ic * 512:(ic + 1) * 512],
                                           ps_a[:, b * 512:(b + 1) * 512],
                                           gates[:, t, e_:e_ + 1],
                                           x_sb[:, t, ic * 512:(ic + 1) * 512],
                                           ALU.mult, ALU.add),
                                       reads=(f"pA{b}", f"gates{t}", f"x{t}"),
                                       writes=(f"x{t}",))
                    if idx + 2 < n_gran:
                        issue_gran(idx + 2)
                ctl.barrier()
                ctl.flush()

        def attn():
            actx = ExitStack()
            with actx:
                asb = lambda *a: actx.enter_context(nc.sbuf_tensor(*a))
                q_fm = asb("q_fm", [128, c.D // 256, c.T], BF16)
                k_fm = asb("k_fm", [128, c.D // 256, c.T], BF16)
                v_tm = asb("v_tm", [128, c.TT, c.D], BF16)
                pctx = ExitStack()
                psb = lambda *a: pctx.enter_context(nc.sbuf_tensor(*a))
                ln2 = psb("ln2", [128, 2, c.DC, 512], BF16)
                sqa = psb("sqa", [128, c.D], F32)
                lntma = psb("lntma", [128, 2, c.D], BF16)
                wqk = psb("wqk", [128, c.DC, c.D], BF16)
                wv = psb("wv", [128, c.DC, c.D], BF16)

                ctl.dma("gpsimd", "d_ini",
                        [lambda e: e.dma_start(
                            out=wqk[:].rearrange("p dc k -> p (dc k)"), in_=wqks),
                         lambda e: e.dma_start(
                            out=wv[:].rearrange("p dc k -> p (dc k)"), in_=wvs)],
                        writes=("wqk", "wv"))

                for blk in range(c.NTB):
                    r = blk % 2
                    for tt in range(4):
                        t = blk * 4 + tt
                        ctl.op("scalar",
                               lambda e, t=t: e.activation(
                                   sqa[:], x_sb[:, t, :], AF.Square,
                                   accum_out=st_qq[:, t:t + 1]),
                               reads=(f"x{t}",), writes=("sqa", f"qq{t}"))
                        ctl.op("vector",
                               lambda e, t=t: e.reduce_sum(
                                   st_s[:, t:t + 1], x_sb[:, t, :], axis=AX.X),
                               reads=(f"x{t}",), writes=(f"s{t}",))
                        ctl.op("vector",
                               lambda e, t=t: e.tensor_scalar(
                                   st_nm[:, t:t + 1], st_s[:, t:t + 1],
                                   -1.0 / c.D, None, ALU.mult),
                               reads=(f"s{t}",), writes=(f"nm{t}",))
                        ctl.op("vector",
                               lambda e, t=t: e.tensor_mul(
                                   st_m2[:, t:t + 1], st_nm[:, t:t + 1],
                                   st_nm[:, t:t + 1]),
                               reads=(f"nm{t}",), writes=(f"m2{t}",))
                        ctl.op("vector",
                               lambda e, t=t: e.scalar_tensor_tensor(
                                   st_var[:, t:t + 1], st_qq[:, t:t + 1],
                                   1.0 / c.D, st_m2[:, t:t + 1],
                                   ALU.mult, ALU.subtract),
                               reads=(f"qq{t}", f"m2{t}"), writes=(f"var{t}",))
                        ctl.op("scalar",
                               lambda e, t=t: e.activation(
                                   st_std[:, t:t + 1], st_var[:, t:t + 1],
                                   AF.Sqrt, bias=epsc[:]),
                               reads=(f"var{t}", "eps"), writes=(f"std{t}",))
                        ctl.op("vector",
                               lambda e, t=t: e.reciprocal(
                                   st_rs[:, t:t + 1], st_std[:, t:t + 1]),
                               reads=(f"std{t}",), writes=(f"rs{t}",))
                        rr = t % 2
                        ctl.op("vector",
                               lambda e, t=t, rr=rr: e.tensor_scalar(
                                   lntma[:, rr, :], x_sb[:, t, :],
                                   st_nm[:, t:t + 1], st_rs[:, t:t + 1],
                                   ALU.add, ALU.mult),
                               reads=(f"x{t}", f"nm{t}", f"rs{t}"),
                               writes=(f"lntma{rr}",))
                        for hb, pf in ((0, ps_f0), (1, ps_f1)):
                            ctl.group("tensor",
                                      [(lambda e, rr=rr, dc=dc, hb=hb, pf=pf:
                                        e.matmul(
                                          pf[:, (dc - hb * 4) * 128:
                                             (dc - hb * 4 + 1) * 128],
                                          lntma[:, rr, dc * 128:(dc + 1) * 128],
                                          idn[:], start=True, stop=True))
                                       for dc in range(hb * 4, hb * 4 + 4)],
                                      reads=(f"lntma{rr}", "idn"),
                                      writes=(f"pf{hb}",))
                            ctl.op("scalar",
                                   lambda e, r=r, tt=tt, hb=hb, pf=pf: e.copy(
                                       ln2[:, r, hb * 4:hb * 4 + 4,
                                           tt * 128:(tt + 1) * 128],
                                       pf[:].rearrange("p (dc k) -> p dc k",
                                                       k=128)),
                                   reads=(f"pf{hb}",), writes=(f"ln2_{r}_{tt}",))
                    lnreads = tuple(f"ln2_{r}_{tt}" for tt in range(4))
                    for uc in range(c.D // 256):
                        for half, (dst, dkey) in enumerate(
                                ((q_fm, "qfm"), (k_fm, "kfm"))):
                            ctl.group("tensor",
                                      [(lambda e, uc=uc, half=half, dc=dc, r=r:
                                        e.matmul(
                                          ps_c[:],
                                          wqk[:, dc,
                                              half * 512 + uc * 128:
                                              half * 512 + (uc + 1) * 128],
                                          ln2[:, r, dc, :],
                                          start=(dc == 0), stop=(dc == c.DC - 1)))
                                       for dc in range(c.DC)],
                                      reads=lnreads + ("wqk",), writes=("pC",))
                            if half == 0:
                                ctl.op("vector",
                                       lambda e, uc=uc, blk=blk, dst=dst:
                                       e.tensor_scalar(
                                           dst[:, uc, blk * 512:(blk + 1) * 512],
                                           ps_c[:], 0.125, None, ALU.mult),
                                       reads=("pC",), writes=(dkey,))
                            else:
                                ctl.op("scalar",
                                       lambda e, uc=uc, blk=blk, dst=dst: e.copy(
                                           dst[:, uc, blk * 512:(blk + 1) * 512],
                                           ps_c[:]),
                                       reads=("pC",), writes=(dkey,))
                    for tt in range(4):
                        t = blk * 4 + tt
                        for ic in range(2):
                            ctl.group("tensor",
                                      [(lambda e, tt=tt, ic=ic, dc=dc, r=r:
                                        e.matmul(
                                          ps_d[:],
                                          ln2[:, r, dc, tt * 128:(tt + 1) * 128],
                                          wv[:, dc, ic * 512:(ic + 1) * 512],
                                          start=(dc == 0), stop=(dc == c.DC - 1)))
                                       for dc in range(c.DC)],
                                      reads=lnreads + ("wv",), writes=("pD",))
                            ctl.op("scalar",
                                   lambda e, t=t, ic=ic: e.copy(
                                       v_tm[:, t, ic * 512:(ic + 1) * 512],
                                       ps_d[:]),
                                   reads=("pD",), writes=("vtm",))

                ctl.barrier()
                pctx.close()
                wo = asb("wo", [128, c.DC, c.D], BF16)
                bbc = asb("bbc", [128, c.D], BF16)
                strip = asb("strip", [128, 2, STRW], BF16)
                pbuf = asb("pbuf", [128, 2, c.T], BF16)
                sc_sb = asb("sc_sb", [128, 2, c.T], F32)
                dgz = asb("dgz", [128, 2, 128], BF16)
                otmp2 = asb("otmp2", [128, 512], BF16)
                p_t = asb("p_t", [128, c.TT, 512], BF16)
                aout = asb("aout", [128, c.NH, 512], BF16)
                otmp = asb("otmp", [128, 512], BF16)
                ctl.dma("gpsimd", "d_ini",
                        [lambda e: e.dma_start(
                            out=wo[:].rearrange("p dc k -> p (dc k)"), in_=wos)],
                        writes=("wo",))
                ctl.dma("gpsimd", "d_bb",
                        [lambda e: e.dma_start(
                            out=bbc[:],
                            in_=b_out_t[None, :].broadcast_to([128, c.D]))],
                        writes=("bbc",))

                def emit_S(j, t1b, h, tt, half):
                    uc, po = h // 2, (h % 2) * 64
                    t1 = t1b * 4 + tt
                    nk = c.T // 512
                    kbs = tuple(range((nk + 1) // 2)) if half == 0 else tuple(
                        range((nk + 1) // 2, nk))
                    if not kbs:
                        return
                    ctl.group("tensor",
                              [(lambda e, kb=kb, uc=uc, po=po, t1=t1: e.matmul(
                                  ps_a[:, kb * 512:(kb + 1) * 512],
                                  q_fm[po:po + 64, uc, t1 * 128:(t1 + 1) * 128],
                                  k_fm[po:po + 64, uc, kb * 512:(kb + 1) * 512],
                                  start=True, stop=True))
                               for kb in kbs],
                              reads=("qfm", "kfm"),
                              writes=tuple(f"pA{kb}" for kb in kbs))

                def emit_B(j, t1b, h, tt, half):
                    r, spar = j % 2, (j // 4) % 2
                    so = 384 - 128 * tt
                    nk = c.T // 512
                    k0 = (nk + 1) // 2
                    kbs = tuple(range(k0)) if half == 0 else tuple(range(k0, nk))
                    if not kbs:
                        return
                    lo = 0 if half == 0 else k0 * 512
                    HW = (k0 * 512 - lo) if half == 0 else (c.T - lo)
                    ctl.op("vector",
                           lambda e, r=r, spar=spar, so=so, lo=lo, HW=HW:
                           e.tensor_add(
                               sc_sb[:, r, lo:lo + HW], ps_a[:, lo:lo + HW],
                               strip[:, spar, so + lo:so + lo + HW]),
                           reads=(f"strip{spar}",) + tuple(
                               f"pA{kb}" for kb in kbs),
                           writes=(f"scb{r}h{half}",))

                def emit_chc(j, h):
                    r = j % 2
                    ctl.op("vector",
                           lambda e, r=r: e.reduce_max(
                               negb[:], sc_sb[:, r, :], axis=AX.X, negate=True),
                           reads=(f"scb{r}h0", f"scb{r}h1"), writes=("smaxh",))
                    ctl.op("tensor",
                           lambda e: e.matmul(ps_f0[0:1, 0:128], negb[:], idn[:],
                                              start=True, stop=True),
                           reads=("smaxh", "idn"), writes=("pf0",))
                    ctl.op("vector",
                           lambda e: e.tensor_reduce(
                               sc1t[:], ps_f0[0:1, 0:128], axis=AX.X, op=ALU.min),
                           reads=("pf0",), writes=("sc1t",))
                    ctl.op("tensor",
                           lambda e: e.matmul(ps_f0[:, 128:129], onesf[:],
                                              sc1t[:], start=True, stop=True),
                           reads=("onesf", "sc1t"), writes=("pf0",))
                    ctl.op("vector",
                           lambda e, h=h: e.tensor_scalar(
                               chC[:, h:h + 1], ps_f0[:, 128:129], -4.0, None,
                               ALU.add),
                           reads=("pf0",), writes=(f"chC{h}",))

                def emit_E(j, h):
                    r = j % 2
                    ctl.op("scalar",
                           lambda e, r=r, h=h: e.activation(
                               pbuf[:, r, :], sc_sb[:, r, :], AF.Exp,
                               bias=chC[:, h:h + 1], accum_out=g_z[:, r:r + 1]),
                           reads=(f"scb{r}h0", f"scb{r}h1", f"chC{h}"),
                           writes=(f"pbuf{r}", f"sz{r}"))

                def emit_norm(j):
                    r = j % 2
                    ctl.op("vector",
                           lambda e, r=r: e.reciprocal(g_rz[:, r:r + 1],
                                                       g_z[:, r:r + 1]),
                           reads=(f"sz{r}",), writes=(f"srz{r}",))
                    ctl.op("vector",
                           lambda e, r=r: e.tensor_scalar(
                               dgz[:, r, :], idn[:], g_rz[:, r:r + 1],
                               None, ALU.mult),
                           reads=("idn", f"srz{r}",), writes=(f"dgz{r}",))

                def emit_T(j, tt):
                    r = j % 2
                    for w in range(c.TT // 4):
                        pf = (ps_f0, ps_f1)[w % 2]
                        ctl.group("tensor",
                                  [(lambda e, r=r, w=w, q=q, pf=pf: e.matmul(
                                      pf[:, q * 128:(q + 1) * 128],
                                      pbuf[:, r, (4 * w + q) * 128:
                                           (4 * w + q + 1) * 128],
                                      dgz[:, r, :], start=True, stop=True))
                                   for q in range(4)],
                                  reads=(f"pbuf{r}", f"dgz{r}"),
                                  writes=(f"pf{w % 2}",))
                        eng = "scalar" if w % 2 == 0 else "vector"
                        if eng == "scalar":
                            fn = (lambda e, w=w, tt=tt, pf=pf: e.copy(
                                p_t[:, 4 * w:4 * w + 4, tt * 128:(tt + 1) * 128],
                                pf[:].rearrange("p (a b) -> p a b", b=128)))
                        else:
                            fn = (lambda e, w=w, tt=tt, pf=pf: e.tensor_copy(
                                p_t[:, 4 * w:4 * w + 4, tt * 128:(tt + 1) * 128],
                                pf[:].rearrange("p (a b) -> p a b", b=128)))
                        ctl.op(eng, fn, reads=(f"pf{w % 2}",), writes=("ptb",))

                def emit_av(h):
                    ctl.group("tensor",
                              [(lambda e, t2c=t2c, h=h: e.matmul(
                                  ps_c[:], v_tm[:, t2c, h * 128:(h + 1) * 128],
                                  p_t[:, t2c, :],
                                  start=(t2c == 0), stop=(t2c == c.TT - 1)))
                               for t2c in range(c.TT)],
                              reads=("ptb", "vtm"), writes=("pC",))
                    ctl.op("scalar",
                           lambda e, h=h: e.copy(aout[:, h, :], ps_c[:]),
                           reads=("pC",), writes=(f"aout{h}",))

                def emit_outproj(t1b):
                    areads = tuple(f"aout{h}" for h in range(c.NH))
                    for tt in range(4):
                        t = t1b * 4 + tt
                        for ic in range(2):
                            par = (2 * tt + ic) % 2
                            pso = (ps_d, ps_c)[par]
                            pkey = ("pD", "pC")[par]
                            ot = (otmp, otmp2)[par]
                            okey = ("otmp", "otmp2")[par]
                            ctl.group("tensor",
                                      [(lambda e, hh=hh, tt=tt, ic=ic, pso=pso:
                                        e.matmul(
                                          pso[:],
                                          aout[:, hh, tt * 128:(tt + 1) * 128],
                                          wo[:, hh, ic * 512:(ic + 1) * 512],
                                          start=(hh == 0), stop=(hh == c.NH - 1)))
                                       for hh in range(c.NH)],
                                      reads=areads + ("wo",), writes=(pkey,))
                            ctl.op("vector",
                                   lambda e, ic=ic, pso=pso, ot=ot: e.tensor_add(
                                       ot[:], pso[:],
                                       bbc[:, ic * 512:(ic + 1) * 512]),
                                   reads=(pkey, "bbc"), writes=(okey,))
                            ctl.op("vector",
                                   lambda e, t=t, ic=ic, ot=ot: e.tensor_add(
                                       x_sb[:, t, ic * 512:(ic + 1) * 512],
                                       x_sb[:, t, ic * 512:(ic + 1) * 512],
                                       ot[:]),
                                   reads=(okey, f"x{t}"), writes=(f"x{t}",))

                jobs = [(t1b, h, tt) for t1b in range(c.NTB)
                        for h in range(c.NH) for tt in range(4)]
                pending = []

                def drain_one():
                    pj, pt1b, ph, ptt = pending.pop(0)
                    emit_T(pj, ptt)
                    if ptt == 3:
                        emit_av(ph)
                        if ph == c.NH - 1:
                            emit_outproj(pt1b)

                prev = None
                for j, (t1b, h, tt) in enumerate(jobs):
                    if tt == 0:
                        spar = (j // 4) % 2
                        off = (c.NTB - 1 - t1b) * 512
                        ctl.dma("gpsimd", f"d_s{spar}",
                                [(lambda e, h=h, off=off, spar=spar: e.dma_start(
                                    out=strip[:, spar, :],
                                    in_=strips[h, :, off:off + STRW]))],
                                writes=(f"strip{spar}",))
                    emit_S(j, t1b, h, tt, 0)
                    emit_B(j, t1b, h, tt, 0)
                    emit_S(j, t1b, h, tt, 1)
                    if prev is not None:
                        emit_norm(prev)
                    emit_B(j, t1b, h, tt, 1)
                    if t1b == 0 and tt == 0:
                        emit_chc(j, h)
                    if len(pending) >= 2:
                        drain_one()
                    emit_E(j, h)
                    pending.append((j, t1b, h, tt))
                    prev = j
                emit_norm(prev)
                while pending:
                    drain_one()
                sqv = sc_sb[:].rearrange("p a b -> p (a b)")[:, 0:c.D]
                ln_stats(sqv, "scb0h0")
                ctl.barrier()
                ctl.flush()

        def dump_x(dst):
            d3 = dst.rearrange("(t p) d -> t p d", p=128)
            ctl.dma("sync", "d_dbg",
                    [(lambda e, t=t: e.dma_start(out=d3[t], in_=x_sb[:, t, :]))
                     for t in range(c.TT)],
                    reads=tuple(f"x{t}" for t in range(c.TT)))
            ctl.barrier()

        ffn(0)
        if dbg:
            dump_x(dbg_x1)
        attn()
        if dbg:
            dump_x(dbg_x2)
            ctl.dma("sync", "d_dbg",
                    [lambda e: e.dma_start(out=dbg_chc, in_=chC[:])],
                    reads=tuple(f"chC{h}" for h in range(c.NH)))
            ctl.barrier()
        ffn(1)

        y3 = y_out.rearrange("(t p) d -> t p d", p=128)
        ctl.dma("sync", "d_out",
                [(lambda e, t=t: e.dma_start(out=y3[t], in_=x_sb[:, t, :]))
                 for t in range(c.TT)],
                reads=tuple(f"x{t}" for t in range(c.TT)))
        ctl.barrier()
        ctl.flush()
    return nc


def _prep_host(cfg, W_qk, W_v, W_out, b_out, conv, router_w, w1, w2):
    c = cfg
    JGW = c.JCG * 128

    def rows_tiled(m):
        D, K = m.shape
        return np.ascontiguousarray(
            m.reshape(D // 128, 128, K).transpose(1, 0, 2).reshape(128, -1))

    w1s = np.empty((c.E, c.NG, 128, c.DC * JGW), dtype=np.float32)
    w2s = np.empty((c.E, c.NG, 128, c.JCG * c.D), dtype=np.float32)
    for e in range(c.E):
        w1t = np.ascontiguousarray(w1[e].T)
        w2t = np.ascontiguousarray(w2[e].T)
        for g in range(c.NG):
            blk = w1t[:, g * JGW:(g + 1) * JGW]
            w1s[e, g] = rows_tiled(blk)
            blk2 = w2t[g * JGW:(g + 1) * JGW, :]
            w2s[e, g] = rows_tiled(blk2)

    wqks = rows_tiled(np.ascontiguousarray(W_qk))
    wvs = rows_tiled(np.ascontiguousarray(W_v))
    wos = rows_tiled(np.ascontiguousarray(W_out))
    wrs = rows_tiled(np.ascontiguousarray(router_w.T))

    SFULL = c.T + 1920
    base = c.T - 128
    cols = np.arange(SFULL)
    p = np.arange(128)
    d = cols[None, :] - base - p[:, None]
    idx = np.clip(d, -c.MAXPOS, c.MAXPOS) + c.MAXPOS
    strips = np.ascontiguousarray(conv[:, idx]).astype(np.float32)

    return dict(w1s=w1s, w2s=w2s, wqks=wqks, wvs=wvs, wos=wos, wrs=wrs,
                b_out_t=np.ascontiguousarray(b_out).astype(np.float32),
                strips=strips)


def kernel(L0_attn_x, L0_attn_mask, L0_attn_W_qk, L0_attn_W_v, L0_attn_W_out,
           L0_attn_b_out, L0_attn_conv, L0_ffn_router_w, L0_ffn_w1, L0_ffn_w2):
    cfg = CFG()
    x = np.ascontiguousarray(np.asarray(L0_attn_x, dtype=np.float32))
    B = x.shape[0]
    shared = _prep_host(cfg,
                        np.asarray(L0_attn_W_qk, np.float32),
                        np.asarray(L0_attn_W_v, np.float32),
                        np.asarray(L0_attn_W_out, np.float32),
                        np.asarray(L0_attn_b_out, np.float32),
                        np.asarray(L0_attn_conv, np.float32),
                        np.asarray(L0_ffn_router_w, np.float32),
                        np.asarray(L0_ffn_w1, np.float32),
                        np.asarray(L0_ffn_w2, np.float32))
    nc = build_program(cfg)
    in_maps = [dict(shared, x_in=np.ascontiguousarray(x[b])) for b in range(B)]
    res = run_bass_kernel_spmd(nc, in_maps, core_ids=list(range(B)))
    out = np.stack([res.results[b]["y_out"] for b in range(B)], axis=0)
    return out


# version 20
# speedup vs baseline: 1.0628x; 1.0628x over previous
import numpy as np
import ml_dtypes
from contextlib import ExitStack

import concourse.bass as bass
import concourse.mybir as mybir
from concourse.bass_utils import run_bass_kernel_spmd

F32 = mybir.dt.float32
BF16 = mybir.dt.bfloat16
AF = mybir.ActivationFunctionType
ALU = mybir.AluOpType
AX = mybir.AxisListType

BF = ml_dtypes.bfloat16


class CFG:
    def __init__(self, T=2048, D=1024, H=4096, E=8, NH=8, DQK=64, TB=256, JCG=8):
        self.T, self.D, self.H, self.E, self.NH, self.DQK, self.TB, self.JCG = (
            T, D, H, E, NH, DQK, TB, JCG)
        self.DV = D // NH
        self.TT = T // 128
        self.DC = D // 128
        self.JC = H // 128
        self.NG = self.JC // JCG
        self.TBN = T // TB
        self.NTB = T // 512
        self.MAXPOS = 10
        assert TB % 128 == 0 and self.TB // 128 == 2


ENGINES = ("sync", "scalar", "vector", "tensor", "gpsimd")


class Ctl:

    def __init__(self, nc, ctx):
        self.nc = nc
        self.q = {e: [] for e in ENGINES}
        self.sems = {}
        self.ticks = {}
        self.waited = {e: {} for e in ENGINES}
        self.writer = {}
        self.readers = {}
        self._ctx = ctx
        for name in ("t_sync", "t_scalar", "t_vector", "t_tensor", "t_gpsimd"):
            self._add_sem(name)

    def _add_sem(self, name):
        if name not in self.sems:
            h = self._ctx.enter_context(self.nc.semaphore(name))
            self.sems[name] = h
            self.ticks[name] = 0

    def _emit_wait(self, eng, semname, val):
        if val <= 0:
            return
        if self.waited[eng].get(semname, 0) >= val:
            return
        assert val <= self.ticks[semname], (
            f"{eng} waits {semname}>={val}, only {self.ticks[semname]} produced")
        self.waited[eng][semname] = val
        h = self.sems[semname]
        self.q[eng].append(lambda e, h=h, v=val: e.wait_ge(h, v))

    def _dep_waits(self, eng, reads, writes):
        for b in tuple(reads) + tuple(writes):
            w = self.writer.get(b)
            if w:
                self._emit_wait(eng, w[0], w[1])
        for b in writes:
            for sname, v in self.readers.get(b, {}).items():
                self._emit_wait(eng, sname, v)

    def _record(self, sem, val, reads, writes):
        for b in writes:
            self.writer[b] = (sem, val)
            self.readers[b] = {}
        for b in reads:
            self.readers.setdefault(b, {})[sem] = val

    def op(self, eng, fn, reads=(), writes=()):
        self._dep_waits(eng, reads, writes)
        sem = "t_" + eng
        self.ticks[sem] += 1
        val = self.ticks[sem]
        h = self.sems[sem]
        self.q[eng].append(lambda e, fn=fn, h=h: fn(e).then_inc(h, 1))
        self._record(sem, val, reads, writes)
        return sem, val

    def dma(self, eng, sem, fns, reads=(), writes=()):
        self._add_sem(sem)
        self._dep_waits(eng, reads, writes)
        h = self.sems[sem]
        for fn in fns:
            self.ticks[sem] += 16
            self.q[eng].append(lambda e, fn=fn, h=h: fn(e).then_inc(h, 16))
        self._record(sem, self.ticks[sem], reads, writes)
        return sem, self.ticks[sem]

    def group(self, eng, fns, reads=(), writes=()):
        self._dep_waits(eng, reads, writes)
        sem = "t_" + eng
        self.ticks[sem] += 1
        val = self.ticks[sem]
        h = self.sems[sem]
        for fn in fns[:-1]:
            self.q[eng].append(fn)
        self.q[eng].append(lambda e, fn=fns[-1], h=h: fn(e).then_inc(h, 1))
        self._record(sem, val, reads, writes)
        return sem, val

    def barrier(self):
        snap = {k: v for k, v in self.ticks.items() if v > 0}
        for e in ENGINES:
            for sname, v in snap.items():
                self._emit_wait(e, sname, v)

    def flush(self):
        qs = self.q
        self.q = {e: [] for e in ENGINES}
        with self.nc.Block() as block:
            @block.sync
            def _(e):
                for f in qs["sync"]:
                    f(e)

            @block.scalar
            def _(e):
                for f in qs["scalar"]:
                    f(e)

            @block.vector
            def _(e):
                for f in qs["vector"]:
                    f(e)

            @block.tensor
            def _(e):
                for f in qs["tensor"]:
                    f(e)

            @block.gpsimd
            def _(e):
                for f in qs["gpsimd"]:
                    f(e)


def build_program(cfg: CFG, dbg=False):
    c = cfg
    nc = bass.Bass("TRN2", target_bir_lowering=False)

    x_in = nc.dram_tensor("x_in", [c.T, c.D], F32, kind="ExternalInput").ap()
    y_out = nc.dram_tensor("y_out", [c.T, c.D], F32, kind="ExternalOutput").ap()
    w1s = nc.dram_tensor("w1s", [c.E, c.NG, 128, c.DC * c.JCG * 128], F32,
                         kind="ExternalInput").ap()
    w2s = nc.dram_tensor("w2s", [c.E, c.NG, 128, c.JCG * c.D], F32,
                         kind="ExternalInput").ap()
    wqks = nc.dram_tensor("wqks", [128, c.DC * c.D], F32, kind="ExternalInput").ap()
    wvs = nc.dram_tensor("wvs", [128, c.DC * c.D], F32, kind="ExternalInput").ap()
    wos = nc.dram_tensor("wos", [128, c.DC * c.D], F32, kind="ExternalInput").ap()
    wrs = nc.dram_tensor("wrs", [128, c.DC * c.E], F32, kind="ExternalInput").ap()
    b_out_t = nc.dram_tensor("b_out_t", [c.D], F32, kind="ExternalInput").ap()
    STRW = c.T + 384
    SFULL = c.T + 1920
    strips = nc.dram_tensor("strips", [c.NH, 128, SFULL], F32,
                            kind="ExternalInput").ap()
    if dbg:
        dbg_x1 = nc.dram_tensor("dbg_x1", [c.T, c.D], F32, kind="ExternalOutput").ap()
        dbg_x2 = nc.dram_tensor("dbg_x2", [c.T, c.D], F32, kind="ExternalOutput").ap()
        dbg_chc = nc.dram_tensor("dbg_chc", [128, c.NH], F32, kind="ExternalOutput").ap()
    ident_d = nc.inline_tensor(np.eye(128, dtype=BF), name="ident_bf").ap()
    onesf_d = nc.inline_tensor(np.ones((1, 128), dtype=BF), name="ones_b").ap()

    JGW = c.JCG * 128

    ctx = ExitStack()
    with ctx:
        ctl = Ctl(nc, ctx)
        sb = lambda *a: ctx.enter_context(nc.sbuf_tensor(*a))
        psum = lambda *a: ctx.enter_context(nc.psum_tensor(*a))

        x_sb = sb("x_sb", [128, c.TT, c.D], F32)
        idn = sb("idn", [128, 128], BF16)
        negb = sb("negb", [128, 1], BF16)
        onesf = sb("onesf", [1, 128], BF16)
        chC = sb("chC", [128, c.NH], F32)
        sc1t = sb("sc1t", [1, 1], BF16)
        rtr = sb("rtr", [128, c.DC, c.E], BF16)
        epsc = sb("epsc", [128, 1], F32)
        st_s = sb("st_s", [128, c.TT], F32)
        st_nm = sb("st_nm", [128, c.TT], F32)
        st_qq = sb("st_qq", [128, c.TT], F32)
        st_m2 = sb("st_m2", [128, c.TT], F32)
        st_var = sb("st_var", [128, c.TT], F32)
        st_std = sb("st_std", [128, c.TT], F32)
        st_rs = sb("st_rs", [128, c.TT], F32)
        g_e = sb("g_e", [128, 2, c.E], F32)
        g_z = sb("g_z", [128, 2], F32)
        g_m1 = sb("g_m1", [128, 2], F32)
        g_nm = sb("g_nm", [128, 2], F32)
        g_k = sb("g_k", [128, 2, c.E], F32)
        g_m2 = sb("g_m2", [128, 2], F32)
        g_rz = sb("g_rz", [128, 2], F32)

        ps_a = psum("ps_a", [128, 2048], F32)
        ps_c = psum("ps_c", [128, 512], F32)
        ps_d = psum("ps_d", [128, 512], F32)
        ps_f0 = psum("ps_f0", [128, 512], F32)
        ps_f1 = psum("ps_f1", [128, 512], F32)

        x3 = x_in.rearrange("(t p) d -> t p d", p=128)
        nb = max(1, c.TT // 4)
        for bi in range(0, c.TT, nb):
            eng = ("sync", "scalar")[(bi // nb) % 2]
            ctl.dma(eng, f"d_x{bi}",
                    [(lambda e, t=t: e.dma_start(out=x_sb[:, t, :], in_=x3[t]))
                     for t in range(bi, min(c.TT, bi + nb))],
                    writes=tuple(f"x{t}" for t in range(bi, min(c.TT, bi + nb))))
        ctl.dma("gpsimd", "d_ini",
                [lambda e: e.dma_start(out=idn[:], in_=ident_d),
                 lambda e: e.dma_start(out=onesf[:], in_=onesf_d),
                 lambda e: e.dma_start(
                     out=rtr[:].rearrange("p dc k -> p (dc k)"), in_=wrs)],
                writes=("idn", "onesf", "rtr"))
        ctl.op("gpsimd", lambda e: e.memset(epsc[:], 1e-5), writes=("eps",))

        def ln_stats(sq, sqkey, tset=None):
            tset = range(c.TT) if tset is None else tset
            for t in tset:
                ctl.op("scalar",
                       lambda e, t=t: e.activation(sq, x_sb[:, t, :], AF.Square,
                                                   accum_out=st_qq[:, t:t + 1]),
                       reads=(f"x{t}",), writes=(sqkey, f"qq{t}"))
                ctl.op("vector",
                       lambda e, t=t: e.reduce_sum(st_s[:, t:t + 1], x_sb[:, t, :],
                                                   axis=AX.X),
                       reads=(f"x{t}",), writes=(f"s{t}",))
                ctl.op("vector",
                       lambda e, t=t: e.tensor_scalar(
                           st_nm[:, t:t + 1], st_s[:, t:t + 1], -1.0 / c.D, None,
                           ALU.mult),
                       reads=(f"s{t}",), writes=(f"nm{t}",))
                ctl.op("vector",
                       lambda e, t=t: e.tensor_mul(st_m2[:, t:t + 1],
                                                   st_nm[:, t:t + 1],
                                                   st_nm[:, t:t + 1]),
                       reads=(f"nm{t}",), writes=(f"m2{t}",))
                ctl.op("vector",
                       lambda e, t=t: e.scalar_tensor_tensor(
                           st_var[:, t:t + 1], st_qq[:, t:t + 1], 1.0 / c.D,
                           st_m2[:, t:t + 1], ALU.mult, ALU.subtract),
                       reads=(f"qq{t}", f"m2{t}"), writes=(f"var{t}",))
            for t in tset:
                ctl.op("scalar",
                       lambda e, t=t: e.activation(st_std[:, t:t + 1],
                                                   st_var[:, t:t + 1], AF.Sqrt,
                                                   bias=epsc[:]),
                       reads=(f"var{t}", "eps"), writes=(f"std{t}",))
                ctl.op("vector",
                       lambda e, t=t: e.reciprocal(st_rs[:, t:t + 1],
                                                   st_std[:, t:t + 1]),
                       reads=(f"std{t}",), writes=(f"rs{t}",))
        def ln_and_fm(ln_fm, lnkey, lntm, tset=None):
            tset = range(c.TT) if tset is None else tset
            for t in tset:
                r = t % 2
                ctl.op("vector",
                       lambda e, t=t, r=r: e.tensor_scalar(
                           lntm[:, r, :], x_sb[:, t, :], st_nm[:, t:t + 1],
                           st_rs[:, t:t + 1], ALU.add, ALU.mult),
                       reads=(f"x{t}", f"nm{t}", f"rs{t}"), writes=(f"lntm{r}",))
                for hb, pf in ((0, ps_f0), (1, ps_f1)):
                    ctl.group("tensor",
                              [(lambda e, r=r, dc=dc, hb=hb, pf=pf: e.matmul(
                                  pf[:, (dc - hb * 4) * 128:(dc - hb * 4 + 1) * 128],
                                  lntm[:, r, dc * 128:(dc + 1) * 128], idn[:],
                                  start=True, stop=True))
                               for dc in range(hb * 4, hb * 4 + 4)],
                              reads=(f"lntm{r}", "idn"), writes=(f"pf{hb}",))
                    ctl.op("scalar",
                           lambda e, t=t, hb=hb, pf=pf: e.copy(
                               ln_fm[:, hb * 4:hb * 4 + 4, t * 128:(t + 1) * 128],
                               pf[:].rearrange("p (dc k) -> p dc k", k=128)),
                           reads=(f"pf{hb}",), writes=(f"{lnkey}{t}",))

        def router(ln_fm, lnkey, gates, tset=None):
            tset = range(c.TT) if tset is None else tset
            for t in tset:
                r = t % 2
                psl = (ps_d, ps_c)[r]
                plk = ("pD", "pC")[r]
                ctl.group("tensor",
                          [(lambda e, t=t, dc=dc, psl=psl: e.matmul(
                              psl[:, 0:c.E],
                              ln_fm[:, dc, t * 128:(t + 1) * 128],
                              rtr[:, dc, :],
                              start=(dc == 0), stop=(dc == c.DC - 1)))
                           for dc in range(c.DC)],
                          reads=(f"{lnkey}{t}", "rtr"), writes=(plk,))
                ctl.op("vector",
                       lambda e, r=r, psl=psl: e.reduce_max(
                           g_nm[:, r:r + 1], psl[:, 0:c.E],
                           axis=AX.X, negate=True),
                       reads=(plk,), writes=(f"gnm{r}",))
                ctl.op("scalar",
                       lambda e, r=r, psl=psl: e.activation(
                           g_e[:, r, :], psl[:, 0:c.E],
                           AF.Exp, bias=g_nm[:, r:r + 1],
                           accum_out=g_z[:, r:r + 1]),
                       reads=(plk, f"gnm{r}"), writes=(f"ge{r}", f"gz{r}"))
                ctl.op("vector",
                       lambda e, r=r: e.reduce_max(g_m1[:, r:r + 1], g_e[:, r, :],
                                                   axis=AX.X),
                       reads=(f"ge{r}",), writes=(f"gm1{r}",))
                ctl.op("vector",
                       lambda e, r=r: e.tensor_scalar(g_k[:, r, :], g_e[:, r, :],
                                                      g_m1[:, r:r + 1], None,
                                                      ALU.is_ge),
                       reads=(f"ge{r}", f"gm1{r}"), writes=(f"gk{r}",))
                ctl.op("vector",
                       lambda e, r=r: e.tensor_sub(g_k[:, r, :], g_e[:, r, :],
                                                   g_k[:, r, :]),
                       reads=(f"ge{r}", f"gk{r}"), writes=(f"gk{r}",))
                ctl.op("vector",
                       lambda e, r=r: e.reduce_max(g_m2[:, r:r + 1], g_k[:, r, :],
                                                   axis=AX.X),
                       reads=(f"gk{r}",), writes=(f"gm2{r}",))
                ctl.op("vector",
                       lambda e, r=r: e.tensor_scalar(g_k[:, r, :], g_e[:, r, :],
                                                      g_m2[:, r:r + 1], None,
                                                      ALU.is_ge),
                       reads=(f"ge{r}", f"gm2{r}"), writes=(f"gk{r}",))
                ctl.op("vector",
                       lambda e, r=r: e.tensor_mul(g_k[:, r, :], g_e[:, r, :],
                                                   g_k[:, r, :]),
                       reads=(f"ge{r}", f"gk{r}"), writes=(f"gk{r}",))
                ctl.op("vector",
                       lambda e, r=r: e.reciprocal(g_rz[:, r:r + 1],
                                                   g_z[:, r:r + 1]),
                       reads=(f"gz{r}",), writes=(f"grz{r}",))
                ctl.op("vector",
                       lambda e, t=t, r=r: e.tensor_scalar(
                           gates[:, t, :], g_k[:, r, :], g_rz[:, r:r + 1], 0.5,
                           ALU.mult, ALU.mult),
                       reads=(f"gk{r}", f"grz{r}"), writes=(f"gates{t}",))

        def ffn(call):
            fctx = ExitStack()
            with fctx:
                fsb = lambda *a: fctx.enter_context(nc.sbuf_tensor(*a))
                ln_fm = fsb(f"ln_fm{call}", [128, c.DC, c.T], BF16)
                gates = fsb(f"gates{call}", [128, c.TT, c.E], F32)
                sq = fsb(f"sq{call}", [128, c.D], F32)
                lntm = fsb(f"lntm{call}", [128, 2, c.D], BF16)

                w1b = fsb(f"w1b{call}", [128, 2, c.DC, JGW], BF16)
                w2b = fsb(f"w2b{call}", [128, 2, c.JCG, c.D], BF16)
                h_sb = fsb(f"h_sb{call}", [128, 3, c.TB], BF16)

                LK = f"lf{call}_"
                n_gran = c.E * c.NG

                def issue_gran(idx):
                    e_, g_ = divmod(idx, c.NG)
                    par = idx % 2
                    ctl.dma("gpsimd", f"d_w{par}",
                            [(lambda en, e_=e_, g_=g_, par=par: en.dma_start(
                                out=w1b[:, par, :, :].rearrange(
                                    "p dc j -> p (dc j)"),
                                in_=w1s[e_, g_])),
                             (lambda en, e_=e_, g_=g_, par=par: en.dma_start(
                                out=w2b[:, par, :, :].rearrange(
                                    "p jc i -> p (jc i)"),
                                in_=w2s[e_, g_]))],
                            writes=(f"w1b{par}", f"w2b{par}"))

                issue_gran(0)
                if n_gran > 1:
                    issue_gran(1)
                fast = list(range(c.TB // 128))
                rest = list(range(c.TB // 128, c.TT))
                if call == 0:
                    ln_stats(sq[:], f"sq{call}", fast)
                ln_and_fm(ln_fm, LK, lntm, fast)
                router(ln_fm, LK, gates, fast)
                if call == 0:
                    ln_stats(sq[:], f"sq{call}", rest)
                ln_and_fm(ln_fm, LK, lntm, rest)
                router(ln_fm, LK, gates, rest)

                for idx in range(n_gran):
                    e_, g_ = divmod(idx, c.NG)
                    par = idx % 2
                    for tb in range(c.TBN):
                        t0 = tb * c.TB
                        tiles = (tb * 2, tb * 2 + 1)

                        def h_group(jj, ph, par=par, t0=t0, tiles=tiles):
                            psh = (ps_c, ps_d)[ph]
                            ctl.group("tensor",
                                      [(lambda e, dc=dc, jj=jj, psh=psh, par=par,
                                        t0=t0: e.matmul(
                                          psh[:, 0:c.TB],
                                          w1b[:, par, dc, jj * 128:(jj + 1) * 128],
                                          ln_fm[:, dc, t0:t0 + c.TB],
                                          start=(dc == 0), stop=(dc == c.DC - 1)))
                                       for dc in range(c.DC)],
                                      reads=(f"w1b{par}",) + tuple(
                                          f"{LK}{t}" for t in tiles),
                                      writes=("pC" if ph == 0 else "pD",))

                        def gelu(jj, ph):
                            psh = (ps_c, ps_d)[ph]
                            hs = jj % 3
                            ctl.op("scalar",
                                   lambda e, hs=hs, psh=psh: e.activation(
                                       h_sb[:, hs, :], psh[:, 0:c.TB], AF.Gelu),
                                   reads=("pC" if ph == 0 else "pD",),
                                   writes=(f"hsb{hs}",))

                        def y_group(jj, ph, par=par):
                            hs = jj % 3

                            def mk(tt, ic, b, jj=jj, hs=hs, par=par):
                                return lambda e: e.matmul(
                                    ps_a[:, b * 512:(b + 1) * 512],
                                    h_sb[:, hs, tt * 128:(tt + 1) * 128],
                                    w2b[:, par, jj, ic * 512:(ic + 1) * 512],
                                    start=(jj == 0), stop=(jj == c.JCG - 1))
                            if jj == 0:
                                for tt in range(2):
                                    for ic in range(2):
                                        b = 2 * tt + ic
                                        ctl.group("tensor", [mk(tt, ic, b)],
                                                  reads=(f"hsb{hs}",
                                                         f"w2b{par}"),
                                                  writes=(f"pA{b}",))
                            else:
                                fns = [mk(tt, ic, 2 * tt + ic)
                                       for tt in range(2) for ic in range(2)]
                                ctl.group("tensor", fns,
                                          reads=(f"hsb{hs}", f"w2b{par}"),
                                          writes=("pA0", "pA1", "pA2", "pA3"))

                        h_group(0, 0)
                        gelu(0, 0)
                        for jj in range(1, c.JCG):
                            ph = jj % 2
                            h_group(jj, ph)
                            gelu(jj, ph)
                            y_group(jj - 1, (jj - 1) % 2)
                        y_group(c.JCG - 1, (c.JCG - 1) % 2)

                        for tt in range(2):
                            t = tiles[tt]
                            for ic in range(2):
                                b = 2 * tt + ic
                                ctl.op("vector",
                                       lambda e, t=t, ic=ic, b=b, e_=e_:
                                       e.scalar_tensor_tensor(
                                           x_sb[:, t, ic * 512:(ic + 1) * 512],
                                           ps_a[:, b * 512:(b + 1) * 512],
                                           gates[:, t, e_:e_ + 1],
                                           x_sb[:, t, ic * 512:(ic + 1) * 512],
                                           ALU.mult, ALU.add),
                                       reads=(f"pA{b}", f"gates{t}", f"x{t}"),
                                       writes=(f"x{t}",))
                    if idx + 2 < n_gran:
                        issue_gran(idx + 2)
                ctl.barrier()
                ctl.flush()

        def attn():
            actx = ExitStack()
            with actx:
                asb = lambda *a: actx.enter_context(nc.sbuf_tensor(*a))
                q_fm = asb("q_fm", [128, c.D // 256, c.T], BF16)
                k_fm = asb("k_fm", [128, c.D // 256, c.T], BF16)
                v_tm = asb("v_tm", [128, c.TT, c.D], BF16)
                pctx = ExitStack()
                psb = lambda *a: pctx.enter_context(nc.sbuf_tensor(*a))
                ln2 = psb("ln2", [128, 2, c.DC, 512], BF16)
                sqa = psb("sqa", [128, c.D], F32)
                lntma = psb("lntma", [128, 2, c.D], BF16)
                wqk = psb("wqk", [128, c.DC, c.D], BF16)
                wv = psb("wv", [128, c.DC, c.D], BF16)

                ctl.dma("gpsimd", "d_ini",
                        [lambda e: e.dma_start(
                            out=wqk[:].rearrange("p dc k -> p (dc k)"), in_=wqks),
                         lambda e: e.dma_start(
                            out=wv[:].rearrange("p dc k -> p (dc k)"), in_=wvs)],
                        writes=("wqk", "wv"))

                for blk in range(c.NTB):
                    r = blk % 2
                    for tt in range(4):
                        t = blk * 4 + tt
                        ctl.op("scalar",
                               lambda e, t=t: e.activation(
                                   sqa[:], x_sb[:, t, :], AF.Square,
                                   accum_out=st_qq[:, t:t + 1]),
                               reads=(f"x{t}",), writes=("sqa", f"qq{t}"))
                        ctl.op("vector",
                               lambda e, t=t: e.reduce_sum(
                                   st_s[:, t:t + 1], x_sb[:, t, :], axis=AX.X),
                               reads=(f"x{t}",), writes=(f"s{t}",))
                        ctl.op("vector",
                               lambda e, t=t: e.tensor_scalar(
                                   st_nm[:, t:t + 1], st_s[:, t:t + 1],
                                   -1.0 / c.D, None, ALU.mult),
                               reads=(f"s{t}",), writes=(f"nm{t}",))
                        ctl.op("vector",
                               lambda e, t=t: e.tensor_mul(
                                   st_m2[:, t:t + 1], st_nm[:, t:t + 1],
                                   st_nm[:, t:t + 1]),
                               reads=(f"nm{t}",), writes=(f"m2{t}",))
                        ctl.op("vector",
                               lambda e, t=t: e.scalar_tensor_tensor(
                                   st_var[:, t:t + 1], st_qq[:, t:t + 1],
                                   1.0 / c.D, st_m2[:, t:t + 1],
                                   ALU.mult, ALU.subtract),
                               reads=(f"qq{t}", f"m2{t}"), writes=(f"var{t}",))
                        ctl.op("scalar",
                               lambda e, t=t: e.activation(
                                   st_std[:, t:t + 1], st_var[:, t:t + 1],
                                   AF.Sqrt, bias=epsc[:]),
                               reads=(f"var{t}", "eps"), writes=(f"std{t}",))
                        ctl.op("vector",
                               lambda e, t=t: e.reciprocal(
                                   st_rs[:, t:t + 1], st_std[:, t:t + 1]),
                               reads=(f"std{t}",), writes=(f"rs{t}",))
                        rr = t % 2
                        ctl.op("vector",
                               lambda e, t=t, rr=rr: e.tensor_scalar(
                                   lntma[:, rr, :], x_sb[:, t, :],
                                   st_nm[:, t:t + 1], st_rs[:, t:t + 1],
                                   ALU.add, ALU.mult),
                               reads=(f"x{t}", f"nm{t}", f"rs{t}"),
                               writes=(f"lntma{rr}",))
                        for hb, pf in ((0, ps_f0), (1, ps_f1)):
                            ctl.group("tensor",
                                      [(lambda e, rr=rr, dc=dc, hb=hb, pf=pf:
                                        e.matmul(
                                          pf[:, (dc - hb * 4) * 128:
                                             (dc - hb * 4 + 1) * 128],
                                          lntma[:, rr, dc * 128:(dc + 1) * 128],
                                          idn[:], start=True, stop=True))
                                       for dc in range(hb * 4, hb * 4 + 4)],
                                      reads=(f"lntma{rr}", "idn"),
                                      writes=(f"pf{hb}",))
                            ctl.op("scalar",
                                   lambda e, r=r, tt=tt, hb=hb, pf=pf: e.copy(
                                       ln2[:, r, hb * 4:hb * 4 + 4,
                                           tt * 128:(tt + 1) * 128],
                                       pf[:].rearrange("p (dc k) -> p dc k",
                                                       k=128)),
                                   reads=(f"pf{hb}",), writes=(f"ln2_{r}_{tt}",))
                    lnreads = tuple(f"ln2_{r}_{tt}" for tt in range(4))
                    for uc in range(c.D // 256):
                        for half, (dst, dkey) in enumerate(
                                ((q_fm, "qfm"), (k_fm, "kfm"))):
                            ctl.group("tensor",
                                      [(lambda e, uc=uc, half=half, dc=dc, r=r:
                                        e.matmul(
                                          ps_c[:],
                                          wqk[:, dc,
                                              half * 512 + uc * 128:
                                              half * 512 + (uc + 1) * 128],
                                          ln2[:, r, dc, :],
                                          start=(dc == 0), stop=(dc == c.DC - 1)))
                                       for dc in range(c.DC)],
                                      reads=lnreads + ("wqk",), writes=("pC",))
                            if half == 0:
                                ctl.op("vector",
                                       lambda e, uc=uc, blk=blk, dst=dst:
                                       e.tensor_scalar(
                                           dst[:, uc, blk * 512:(blk + 1) * 512],
                                           ps_c[:], 0.125, None, ALU.mult),
                                       reads=("pC",), writes=(dkey,))
                            else:
                                ctl.op("scalar",
                                       lambda e, uc=uc, blk=blk, dst=dst: e.copy(
                                           dst[:, uc, blk * 512:(blk + 1) * 512],
                                           ps_c[:]),
                                       reads=("pC",), writes=(dkey,))
                    for tt in range(4):
                        t = blk * 4 + tt
                        for ic in range(2):
                            ctl.group("tensor",
                                      [(lambda e, tt=tt, ic=ic, dc=dc, r=r:
                                        e.matmul(
                                          ps_d[:],
                                          ln2[:, r, dc, tt * 128:(tt + 1) * 128],
                                          wv[:, dc, ic * 512:(ic + 1) * 512],
                                          start=(dc == 0), stop=(dc == c.DC - 1)))
                                       for dc in range(c.DC)],
                                      reads=lnreads + ("wv",), writes=("pD",))
                            ctl.op("scalar",
                                   lambda e, t=t, ic=ic: e.copy(
                                       v_tm[:, t, ic * 512:(ic + 1) * 512],
                                       ps_d[:]),
                                   reads=("pD",), writes=("vtm",))

                ctl.barrier()
                pctx.close()
                wo = asb("wo", [128, c.DC, c.D], BF16)
                bbc = asb("bbc", [128, c.D], BF16)
                strip = asb("strip", [128, 2, STRW], BF16)
                pbuf = asb("pbuf", [128, 2, c.T], BF16)
                sc_sb = asb("sc_sb", [128, 2, c.T], F32)
                dgz = asb("dgz", [128, 2, 128], BF16)
                otmp2 = asb("otmp2", [128, 512], BF16)
                p_t = asb("p_t", [128, c.TT, 512], BF16)
                aout = asb("aout", [128, c.NH, 512], BF16)
                otmp = asb("otmp", [128, 512], BF16)
                ctl.dma("gpsimd", "d_ini",
                        [lambda e: e.dma_start(
                            out=wo[:].rearrange("p dc k -> p (dc k)"), in_=wos)],
                        writes=("wo",))
                ctl.dma("gpsimd", "d_bb",
                        [lambda e: e.dma_start(
                            out=bbc[:],
                            in_=b_out_t[None, :].broadcast_to([128, c.D]))],
                        writes=("bbc",))

                def emit_S(j, t1b, h, tt, half):
                    uc, po = h // 2, (h % 2) * 64
                    t1 = t1b * 4 + tt
                    nk = c.T // 512
                    kbs = tuple(range((nk + 1) // 2)) if half == 0 else tuple(
                        range((nk + 1) // 2, nk))
                    if not kbs:
                        return
                    ctl.group("tensor",
                              [(lambda e, kb=kb, uc=uc, po=po, t1=t1: e.matmul(
                                  ps_a[:, kb * 512:(kb + 1) * 512],
                                  q_fm[po:po + 64, uc, t1 * 128:(t1 + 1) * 128],
                                  k_fm[po:po + 64, uc, kb * 512:(kb + 1) * 512],
                                  start=True, stop=True))
                               for kb in kbs],
                              reads=("qfm", "kfm"),
                              writes=tuple(f"pA{kb}" for kb in kbs))

                def emit_B(j, t1b, h, tt, half):
                    r, spar = j % 2, (j // 4) % 2
                    so = 384 - 128 * tt
                    nk = c.T // 512
                    k0 = (nk + 1) // 2
                    kbs = tuple(range(k0)) if half == 0 else tuple(range(k0, nk))
                    if not kbs:
                        return
                    lo = 0 if half == 0 else k0 * 512
                    HW = (k0 * 512 - lo) if half == 0 else (c.T - lo)
                    ctl.op("vector",
                           lambda e, r=r, spar=spar, so=so, lo=lo, HW=HW:
                           e.tensor_add(
                               sc_sb[:, r, lo:lo + HW], ps_a[:, lo:lo + HW],
                               strip[:, spar, so + lo:so + lo + HW]),
                           reads=(f"strip{spar}",) + tuple(
                               f"pA{kb}" for kb in kbs),
                           writes=(f"scb{r}h{half}",))

                def emit_chc(j, h):
                    r = j % 2
                    ctl.op("vector",
                           lambda e, r=r: e.reduce_max(
                               negb[:], sc_sb[:, r, :], axis=AX.X, negate=True),
                           reads=(f"scb{r}h0", f"scb{r}h1"), writes=("smaxh",))
                    ctl.op("tensor",
                           lambda e: e.matmul(ps_f0[0:1, 0:128], negb[:], idn[:],
                                              start=True, stop=True),
                           reads=("smaxh", "idn"), writes=("pf0",))
                    ctl.op("vector",
                           lambda e: e.tensor_reduce(
                               sc1t[:], ps_f0[0:1, 0:128], axis=AX.X, op=ALU.min),
                           reads=("pf0",), writes=("sc1t",))
                    ctl.op("tensor",
                           lambda e: e.matmul(ps_f0[:, 128:129], onesf[:],
                                              sc1t[:], start=True, stop=True),
                           reads=("onesf", "sc1t"), writes=("pf0",))
                    ctl.op("vector",
                           lambda e, h=h: e.tensor_scalar(
                               chC[:, h:h + 1], ps_f0[:, 128:129], -4.0, None,
                               ALU.add),
                           reads=("pf0",), writes=(f"chC{h}",))

                def emit_E(j, h):
                    r = j % 2
                    ctl.op("scalar",
                           lambda e, r=r, h=h: e.activation(
                               pbuf[:, r, :], sc_sb[:, r, :], AF.Exp,
                               bias=chC[:, h:h + 1], accum_out=g_z[:, r:r + 1]),
                           reads=(f"scb{r}h0", f"scb{r}h1", f"chC{h}"),
                           writes=(f"pbuf{r}", f"sz{r}"))

                def emit_norm(j):
                    r = j % 2
                    ctl.op("vector",
                           lambda e, r=r: e.reciprocal(g_rz[:, r:r + 1],
                                                       g_z[:, r:r + 1]),
                           reads=(f"sz{r}",), writes=(f"srz{r}",))
                    ctl.op("vector",
                           lambda e, r=r: e.tensor_scalar(
                               dgz[:, r, :], idn[:], g_rz[:, r:r + 1],
                               None, ALU.mult),
                           reads=("idn", f"srz{r}",), writes=(f"dgz{r}",))

                def emit_T(j, tt):
                    r = j % 2
                    for w in range(c.TT // 4):
                        pf = (ps_f0, ps_f1)[w % 2]
                        ctl.group("tensor",
                                  [(lambda e, r=r, w=w, q=q, pf=pf: e.matmul(
                                      pf[:, q * 128:(q + 1) * 128],
                                      pbuf[:, r, (4 * w + q) * 128:
                                           (4 * w + q + 1) * 128],
                                      dgz[:, r, :], start=True, stop=True))
                                   for q in range(4)],
                                  reads=(f"pbuf{r}", f"dgz{r}"),
                                  writes=(f"pf{w % 2}",))
                        eng = "scalar" if w % 2 == 0 else "vector"
                        if eng == "scalar":
                            fn = (lambda e, w=w, tt=tt, pf=pf: e.copy(
                                p_t[:, 4 * w:4 * w + 4, tt * 128:(tt + 1) * 128],
                                pf[:].rearrange("p (a b) -> p a b", b=128)))
                        else:
                            fn = (lambda e, w=w, tt=tt, pf=pf: e.tensor_copy(
                                p_t[:, 4 * w:4 * w + 4, tt * 128:(tt + 1) * 128],
                                pf[:].rearrange("p (a b) -> p a b", b=128)))
                        ctl.op(eng, fn, reads=(f"pf{w % 2}",), writes=("ptb",))

                def emit_av(h):
                    ctl.group("tensor",
                              [(lambda e, t2c=t2c, h=h: e.matmul(
                                  ps_c[:], v_tm[:, t2c, h * 128:(h + 1) * 128],
                                  p_t[:, t2c, :],
                                  start=(t2c == 0), stop=(t2c == c.TT - 1)))
                               for t2c in range(c.TT)],
                              reads=("ptb", "vtm"), writes=("pC",))
                    ctl.op("scalar",
                           lambda e, h=h: e.copy(aout[:, h, :], ps_c[:]),
                           reads=("pC",), writes=(f"aout{h}",))

                def emit_outproj(t1b):
                    areads = tuple(f"aout{h}" for h in range(c.NH))
                    for tt in range(4):
                        t = t1b * 4 + tt
                        for ic in range(2):
                            par = (2 * tt + ic) % 2
                            pso = (ps_d, ps_c)[par]
                            pkey = ("pD", "pC")[par]
                            ot = (otmp, otmp2)[par]
                            okey = ("otmp", "otmp2")[par]
                            ctl.group("tensor",
                                      [(lambda e, hh=hh, tt=tt, ic=ic, pso=pso:
                                        e.matmul(
                                          pso[:],
                                          aout[:, hh, tt * 128:(tt + 1) * 128],
                                          wo[:, hh, ic * 512:(ic + 1) * 512],
                                          start=(hh == 0), stop=(hh == c.NH - 1)))
                                       for hh in range(c.NH)],
                                      reads=areads + ("wo",), writes=(pkey,))
                            ctl.op("vector",
                                   lambda e, ic=ic, pso=pso, ot=ot: e.tensor_add(
                                       ot[:], pso[:],
                                       bbc[:, ic * 512:(ic + 1) * 512]),
                                   reads=(pkey, "bbc"), writes=(okey,))
                            ctl.op("vector",
                                   lambda e, t=t, ic=ic, ot=ot: e.tensor_add(
                                       x_sb[:, t, ic * 512:(ic + 1) * 512],
                                       x_sb[:, t, ic * 512:(ic + 1) * 512],
                                       ot[:]),
                                   reads=(okey, f"x{t}"), writes=(f"x{t}",))

                jobs = [(t1b, h, tt) for t1b in range(c.NTB)
                        for h in range(c.NH) for tt in range(4)]
                pending = []

                def drain_one():
                    pj, pt1b, ph, ptt = pending.pop(0)
                    emit_T(pj, ptt)
                    if ptt == 3:
                        emit_av(ph)
                        if ph == c.NH - 1:
                            emit_outproj(pt1b)

                prev = None
                for j, (t1b, h, tt) in enumerate(jobs):
                    if tt == 0:
                        spar = (j // 4) % 2
                        off = (c.NTB - 1 - t1b) * 512
                        ctl.dma("gpsimd", f"d_s{spar}",
                                [(lambda e, h=h, off=off, spar=spar: e.dma_start(
                                    out=strip[:, spar, :],
                                    in_=strips[h, :, off:off + STRW]))],
                                writes=(f"strip{spar}",))
                    emit_S(j, t1b, h, tt, 0)
                    emit_B(j, t1b, h, tt, 0)
                    emit_S(j, t1b, h, tt, 1)
                    if prev is not None:
                        emit_norm(prev)
                    emit_B(j, t1b, h, tt, 1)
                    if t1b == 0 and tt == 0:
                        emit_chc(j, h)
                    if len(pending) >= 2:
                        drain_one()
                    emit_E(j, h)
                    pending.append((j, t1b, h, tt))
                    prev = j
                emit_norm(prev)
                while pending:
                    drain_one()
                sqv = sc_sb[:].rearrange("p a b -> p (a b)")[:, 0:c.D]
                ln_stats(sqv, "scb0h0")
                ctl.barrier()
                ctl.flush()

        def dump_x(dst):
            d3 = dst.rearrange("(t p) d -> t p d", p=128)
            ctl.dma("sync", "d_dbg",
                    [(lambda e, t=t: e.dma_start(out=d3[t], in_=x_sb[:, t, :]))
                     for t in range(c.TT)],
                    reads=tuple(f"x{t}" for t in range(c.TT)))
            ctl.barrier()

        ffn(0)
        if dbg:
            dump_x(dbg_x1)
        attn()
        if dbg:
            dump_x(dbg_x2)
            ctl.dma("sync", "d_dbg",
                    [lambda e: e.dma_start(out=dbg_chc, in_=chC[:])],
                    reads=tuple(f"chC{h}" for h in range(c.NH)))
            ctl.barrier()
        ffn(1)

        y3 = y_out.rearrange("(t p) d -> t p d", p=128)
        ctl.dma("sync", "d_out",
                [(lambda e, t=t: e.dma_start(out=y3[t], in_=x_sb[:, t, :]))
                 for t in range(c.TT)],
                reads=tuple(f"x{t}" for t in range(c.TT)))
        ctl.barrier()
        ctl.flush()
    return nc


def _prep_host(cfg, W_qk, W_v, W_out, b_out, conv, router_w, w1, w2):
    c = cfg
    JGW = c.JCG * 128

    def rows_tiled(m):
        D, K = m.shape
        return np.ascontiguousarray(
            m.reshape(D // 128, 128, K).transpose(1, 0, 2).reshape(128, -1))

    w1s = np.empty((c.E, c.NG, 128, c.DC * JGW), dtype=np.float32)
    w2s = np.empty((c.E, c.NG, 128, c.JCG * c.D), dtype=np.float32)
    for e in range(c.E):
        w1t = np.ascontiguousarray(w1[e].T)
        w2t = np.ascontiguousarray(w2[e].T)
        for g in range(c.NG):
            blk = w1t[:, g * JGW:(g + 1) * JGW]
            w1s[e, g] = rows_tiled(blk)
            blk2 = w2t[g * JGW:(g + 1) * JGW, :]
            w2s[e, g] = rows_tiled(blk2)

    wqks = rows_tiled(np.ascontiguousarray(W_qk))
    wvs = rows_tiled(np.ascontiguousarray(W_v))
    wos = rows_tiled(np.ascontiguousarray(W_out))
    wrs = rows_tiled(np.ascontiguousarray(router_w.T))

    SFULL = c.T + 1920
    base = c.T - 128
    cols = np.arange(SFULL)
    p = np.arange(128)
    d = cols[None, :] - base - p[:, None]
    idx = np.clip(d, -c.MAXPOS, c.MAXPOS) + c.MAXPOS
    strips = np.ascontiguousarray(conv[:, idx]).astype(np.float32)

    return dict(w1s=w1s, w2s=w2s, wqks=wqks, wvs=wvs, wos=wos, wrs=wrs,
                b_out_t=np.ascontiguousarray(b_out).astype(np.float32),
                strips=strips)


def kernel(L0_attn_x, L0_attn_mask, L0_attn_W_qk, L0_attn_W_v, L0_attn_W_out,
           L0_attn_b_out, L0_attn_conv, L0_ffn_router_w, L0_ffn_w1, L0_ffn_w2):
    cfg = CFG()
    x = np.ascontiguousarray(np.asarray(L0_attn_x, dtype=np.float32))
    B = x.shape[0]
    shared = _prep_host(cfg,
                        np.asarray(L0_attn_W_qk, np.float32),
                        np.asarray(L0_attn_W_v, np.float32),
                        np.asarray(L0_attn_W_out, np.float32),
                        np.asarray(L0_attn_b_out, np.float32),
                        np.asarray(L0_attn_conv, np.float32),
                        np.asarray(L0_ffn_router_w, np.float32),
                        np.asarray(L0_ffn_w1, np.float32),
                        np.asarray(L0_ffn_w2, np.float32))
    nc = build_program(cfg)
    in_maps = [dict(shared, x_in=np.ascontiguousarray(x[b])) for b in range(B)]
    res = run_bass_kernel_spmd(nc, in_maps, core_ids=list(range(B)))
    out = np.stack([res.results[b]["y_out"] for b in range(B)], axis=0)
    return out


# version 21
# speedup vs baseline: 1.0653x; 1.0024x over previous
import numpy as np
import ml_dtypes
from contextlib import ExitStack

import concourse.bass as bass
import concourse.mybir as mybir
from concourse.bass_utils import run_bass_kernel_spmd

F32 = mybir.dt.float32
BF16 = mybir.dt.bfloat16
AF = mybir.ActivationFunctionType
ALU = mybir.AluOpType
AX = mybir.AxisListType

BF = ml_dtypes.bfloat16


class CFG:
    def __init__(self, T=2048, D=1024, H=4096, E=8, NH=8, DQK=64, TB=256, JCG=8):
        self.T, self.D, self.H, self.E, self.NH, self.DQK, self.TB, self.JCG = (
            T, D, H, E, NH, DQK, TB, JCG)
        self.DV = D // NH
        self.TT = T // 128
        self.DC = D // 128
        self.JC = H // 128
        self.NG = self.JC // JCG
        self.TBN = T // TB
        self.NTB = T // 512
        self.MAXPOS = 10
        assert TB % 128 == 0 and self.TB // 128 == 2


ENGINES = ("sync", "scalar", "vector", "tensor", "gpsimd")


class Ctl:

    def __init__(self, nc, ctx):
        self.nc = nc
        self.q = {e: [] for e in ENGINES}
        self.sems = {}
        self.ticks = {}
        self.waited = {e: {} for e in ENGINES}
        self.writer = {}
        self.readers = {}
        self._ctx = ctx
        for name in ("t_sync", "t_scalar", "t_vector", "t_tensor", "t_gpsimd"):
            self._add_sem(name)

    def _add_sem(self, name):
        if name not in self.sems:
            h = self._ctx.enter_context(self.nc.semaphore(name))
            self.sems[name] = h
            self.ticks[name] = 0

    def _emit_wait(self, eng, semname, val):
        if val <= 0:
            return
        if self.waited[eng].get(semname, 0) >= val:
            return
        assert val <= self.ticks[semname], (
            f"{eng} waits {semname}>={val}, only {self.ticks[semname]} produced")
        self.waited[eng][semname] = val
        h = self.sems[semname]
        self.q[eng].append(lambda e, h=h, v=val: e.wait_ge(h, v))

    def _dep_waits(self, eng, reads, writes):
        for b in tuple(reads) + tuple(writes):
            w = self.writer.get(b)
            if w:
                self._emit_wait(eng, w[0], w[1])
        for b in writes:
            for sname, v in self.readers.get(b, {}).items():
                self._emit_wait(eng, sname, v)

    def _record(self, sem, val, reads, writes):
        for b in writes:
            self.writer[b] = (sem, val)
            self.readers[b] = {}
        for b in reads:
            self.readers.setdefault(b, {})[sem] = val

    def op(self, eng, fn, reads=(), writes=()):
        self._dep_waits(eng, reads, writes)
        sem = "t_" + eng
        self.ticks[sem] += 1
        val = self.ticks[sem]
        h = self.sems[sem]
        self.q[eng].append(lambda e, fn=fn, h=h: fn(e).then_inc(h, 1))
        self._record(sem, val, reads, writes)
        return sem, val

    def dma(self, eng, sem, fns, reads=(), writes=()):
        self._add_sem(sem)
        self._dep_waits(eng, reads, writes)
        h = self.sems[sem]
        for fn in fns:
            self.ticks[sem] += 16
            self.q[eng].append(lambda e, fn=fn, h=h: fn(e).then_inc(h, 16))
        self._record(sem, self.ticks[sem], reads, writes)
        return sem, self.ticks[sem]

    def group(self, eng, fns, reads=(), writes=()):
        self._dep_waits(eng, reads, writes)
        sem = "t_" + eng
        self.ticks[sem] += 1
        val = self.ticks[sem]
        h = self.sems[sem]
        for fn in fns[:-1]:
            self.q[eng].append(fn)
        self.q[eng].append(lambda e, fn=fns[-1], h=h: fn(e).then_inc(h, 1))
        self._record(sem, val, reads, writes)
        return sem, val

    def barrier(self):
        snap = {k: v for k, v in self.ticks.items() if v > 0}
        for e in ENGINES:
            for sname, v in snap.items():
                self._emit_wait(e, sname, v)

    def flush(self):
        qs = self.q
        self.q = {e: [] for e in ENGINES}
        with self.nc.Block() as block:
            @block.sync
            def _(e):
                for f in qs["sync"]:
                    f(e)

            @block.scalar
            def _(e):
                for f in qs["scalar"]:
                    f(e)

            @block.vector
            def _(e):
                for f in qs["vector"]:
                    f(e)

            @block.tensor
            def _(e):
                for f in qs["tensor"]:
                    f(e)

            @block.gpsimd
            def _(e):
                for f in qs["gpsimd"]:
                    f(e)


def build_program(cfg: CFG, dbg=False):
    c = cfg
    nc = bass.Bass("TRN2", target_bir_lowering=False)

    x_in = nc.dram_tensor("x_in", [c.T, c.D], F32, kind="ExternalInput").ap()
    y_out = nc.dram_tensor("y_out", [c.T, c.D], F32, kind="ExternalOutput").ap()
    w1s = nc.dram_tensor("w1s", [c.E, c.NG, 128, c.JCG * c.DC * 128], F32,
                         kind="ExternalInput").ap()
    w2s = nc.dram_tensor("w2s", [c.E, c.NG, 128, c.JCG * c.D], F32,
                         kind="ExternalInput").ap()
    wqks = nc.dram_tensor("wqks", [128, c.DC * c.D], F32, kind="ExternalInput").ap()
    wvs = nc.dram_tensor("wvs", [128, c.DC * c.D], F32, kind="ExternalInput").ap()
    wos = nc.dram_tensor("wos", [128, c.DC * c.D], F32, kind="ExternalInput").ap()
    wrs = nc.dram_tensor("wrs", [128, c.DC * c.E], F32, kind="ExternalInput").ap()
    b_out_t = nc.dram_tensor("b_out_t", [c.D], F32, kind="ExternalInput").ap()
    STRW = c.T + 384
    SFULL = c.T + 1920
    strips = nc.dram_tensor("strips", [c.NH, 128, SFULL], F32,
                            kind="ExternalInput").ap()
    if dbg:
        dbg_x1 = nc.dram_tensor("dbg_x1", [c.T, c.D], F32, kind="ExternalOutput").ap()
        dbg_x2 = nc.dram_tensor("dbg_x2", [c.T, c.D], F32, kind="ExternalOutput").ap()
        dbg_chc = nc.dram_tensor("dbg_chc", [128, c.NH], F32, kind="ExternalOutput").ap()
    ident_d = nc.inline_tensor(np.eye(128, dtype=BF), name="ident_bf").ap()
    onesf_d = nc.inline_tensor(np.ones((1, 128), dtype=BF), name="ones_b").ap()

    JGW = c.JCG * 128

    ctx = ExitStack()
    with ctx:
        ctl = Ctl(nc, ctx)
        sb = lambda *a: ctx.enter_context(nc.sbuf_tensor(*a))
        psum = lambda *a: ctx.enter_context(nc.psum_tensor(*a))

        x_sb = sb("x_sb", [128, c.TT, c.D], F32)
        idn = sb("idn", [128, 128], BF16)
        negb = sb("negb", [128, 1], BF16)
        onesf = sb("onesf", [1, 128], BF16)
        chC = sb("chC", [128, c.NH], F32)
        sc1t = sb("sc1t", [1, 1], BF16)
        rtr = sb("rtr", [128, c.DC, c.E], BF16)
        epsc = sb("epsc", [128, 1], F32)
        st_s = sb("st_s", [128, c.TT], F32)
        st_nm = sb("st_nm", [128, c.TT], F32)
        st_qq = sb("st_qq", [128, c.TT], F32)
        st_m2 = sb("st_m2", [128, c.TT], F32)
        st_var = sb("st_var", [128, c.TT], F32)
        st_std = sb("st_std", [128, c.TT], F32)
        st_rs = sb("st_rs", [128, c.TT], F32)
        g_e = sb("g_e", [128, 2, c.E], F32)
        g_z = sb("g_z", [128, 2], F32)
        g_m1 = sb("g_m1", [128, 2], F32)
        g_nm = sb("g_nm", [128, 2], F32)
        g_k = sb("g_k", [128, 2, c.E], F32)
        g_m2 = sb("g_m2", [128, 2], F32)
        g_rz = sb("g_rz", [128, 2], F32)

        ps_a = psum("ps_a", [128, 2048], F32)
        ps_c = psum("ps_c", [128, 512], F32)
        ps_d = psum("ps_d", [128, 512], F32)
        ps_f0 = psum("ps_f0", [128, 512], F32)
        ps_f1 = psum("ps_f1", [128, 512], F32)

        x3 = x_in.rearrange("(t p) d -> t p d", p=128)
        nb = max(1, c.TT // 4)
        for bi in range(0, c.TT, nb):
            eng = ("sync", "scalar")[(bi // nb) % 2]
            ctl.dma(eng, f"d_x{bi}",
                    [(lambda e, t=t: e.dma_start(out=x_sb[:, t, :], in_=x3[t]))
                     for t in range(bi, min(c.TT, bi + nb))],
                    writes=tuple(f"x{t}" for t in range(bi, min(c.TT, bi + nb))))
        ctl.dma("gpsimd", "d_ini",
                [lambda e: e.dma_start(out=idn[:], in_=ident_d),
                 lambda e: e.dma_start(out=onesf[:], in_=onesf_d),
                 lambda e: e.dma_start(
                     out=rtr[:].rearrange("p dc k -> p (dc k)"), in_=wrs)],
                writes=("idn", "onesf", "rtr"))
        ctl.op("gpsimd", lambda e: e.memset(epsc[:], 1e-5), writes=("eps",))

        def ln_stats(sq, sqkey, tset=None):
            tset = range(c.TT) if tset is None else tset
            for t in tset:
                ctl.op("scalar",
                       lambda e, t=t: e.activation(sq, x_sb[:, t, :], AF.Square,
                                                   accum_out=st_qq[:, t:t + 1]),
                       reads=(f"x{t}",), writes=(sqkey, f"qq{t}"))
                ctl.op("vector",
                       lambda e, t=t: e.reduce_sum(st_s[:, t:t + 1], x_sb[:, t, :],
                                                   axis=AX.X),
                       reads=(f"x{t}",), writes=(f"s{t}",))
                ctl.op("vector",
                       lambda e, t=t: e.tensor_scalar(
                           st_nm[:, t:t + 1], st_s[:, t:t + 1], -1.0 / c.D, None,
                           ALU.mult),
                       reads=(f"s{t}",), writes=(f"nm{t}",))
                ctl.op("vector",
                       lambda e, t=t: e.tensor_mul(st_m2[:, t:t + 1],
                                                   st_nm[:, t:t + 1],
                                                   st_nm[:, t:t + 1]),
                       reads=(f"nm{t}",), writes=(f"m2{t}",))
                ctl.op("vector",
                       lambda e, t=t: e.scalar_tensor_tensor(
                           st_var[:, t:t + 1], st_qq[:, t:t + 1], 1.0 / c.D,
                           st_m2[:, t:t + 1], ALU.mult, ALU.subtract),
                       reads=(f"qq{t}", f"m2{t}"), writes=(f"var{t}",))
            for t in tset:
                ctl.op("scalar",
                       lambda e, t=t: e.activation(st_std[:, t:t + 1],
                                                   st_var[:, t:t + 1], AF.Sqrt,
                                                   bias=epsc[:]),
                       reads=(f"var{t}", "eps"), writes=(f"std{t}",))
                ctl.op("vector",
                       lambda e, t=t: e.reciprocal(st_rs[:, t:t + 1],
                                                   st_std[:, t:t + 1]),
                       reads=(f"std{t}",), writes=(f"rs{t}",))
        def ln_and_fm(ln_fm, lnkey, lntm, tset=None):
            tset = range(c.TT) if tset is None else tset
            for t in tset:
                r = t % 2
                ctl.op("vector",
                       lambda e, t=t, r=r: e.tensor_scalar(
                           lntm[:, r, :], x_sb[:, t, :], st_nm[:, t:t + 1],
                           st_rs[:, t:t + 1], ALU.add, ALU.mult),
                       reads=(f"x{t}", f"nm{t}", f"rs{t}"), writes=(f"lntm{r}",))
                for hb, pf in ((0, ps_f0), (1, ps_f1)):
                    ctl.group("tensor",
                              [(lambda e, r=r, dc=dc, hb=hb, pf=pf: e.matmul(
                                  pf[:, (dc - hb * 4) * 128:(dc - hb * 4 + 1) * 128],
                                  lntm[:, r, dc * 128:(dc + 1) * 128], idn[:],
                                  start=True, stop=True))
                               for dc in range(hb * 4, hb * 4 + 4)],
                              reads=(f"lntm{r}", "idn"), writes=(f"pf{hb}",))
                    ctl.op("scalar",
                           lambda e, t=t, hb=hb, pf=pf: e.copy(
                               ln_fm[:, hb * 4:hb * 4 + 4, t * 128:(t + 1) * 128],
                               pf[:].rearrange("p (dc k) -> p dc k", k=128)),
                           reads=(f"pf{hb}",), writes=(f"{lnkey}{t}",))

        def router(ln_fm, lnkey, gates, tset=None):
            tset = range(c.TT) if tset is None else tset
            for t in tset:
                r = t % 2
                psl = (ps_d, ps_c)[r]
                plk = ("pD", "pC")[r]
                ctl.group("tensor",
                          [(lambda e, t=t, dc=dc, psl=psl: e.matmul(
                              psl[:, 0:c.E],
                              ln_fm[:, dc, t * 128:(t + 1) * 128],
                              rtr[:, dc, :],
                              start=(dc == 0), stop=(dc == c.DC - 1)))
                           for dc in range(c.DC)],
                          reads=(f"{lnkey}{t}", "rtr"), writes=(plk,))
                ctl.op("vector",
                       lambda e, r=r, psl=psl: e.reduce_max(
                           g_nm[:, r:r + 1], psl[:, 0:c.E],
                           axis=AX.X, negate=True),
                       reads=(plk,), writes=(f"gnm{r}",))
                ctl.op("scalar",
                       lambda e, r=r, psl=psl: e.activation(
                           g_e[:, r, :], psl[:, 0:c.E],
                           AF.Exp, bias=g_nm[:, r:r + 1],
                           accum_out=g_z[:, r:r + 1]),
                       reads=(plk, f"gnm{r}"), writes=(f"ge{r}", f"gz{r}"))
                ctl.op("vector",
                       lambda e, r=r: e.reduce_max(g_m1[:, r:r + 1], g_e[:, r, :],
                                                   axis=AX.X),
                       reads=(f"ge{r}",), writes=(f"gm1{r}",))
                ctl.op("vector",
                       lambda e, r=r: e.tensor_scalar(g_k[:, r, :], g_e[:, r, :],
                                                      g_m1[:, r:r + 1], None,
                                                      ALU.is_ge),
                       reads=(f"ge{r}", f"gm1{r}"), writes=(f"gk{r}",))
                ctl.op("vector",
                       lambda e, r=r: e.tensor_sub(g_k[:, r, :], g_e[:, r, :],
                                                   g_k[:, r, :]),
                       reads=(f"ge{r}", f"gk{r}"), writes=(f"gk{r}",))
                ctl.op("vector",
                       lambda e, r=r: e.reduce_max(g_m2[:, r:r + 1], g_k[:, r, :],
                                                   axis=AX.X),
                       reads=(f"gk{r}",), writes=(f"gm2{r}",))
                ctl.op("vector",
                       lambda e, r=r: e.tensor_scalar(g_k[:, r, :], g_e[:, r, :],
                                                      g_m2[:, r:r + 1], None,
                                                      ALU.is_ge),
                       reads=(f"ge{r}", f"gm2{r}"), writes=(f"gk{r}",))
                ctl.op("vector",
                       lambda e, r=r: e.tensor_mul(g_k[:, r, :], g_e[:, r, :],
                                                   g_k[:, r, :]),
                       reads=(f"ge{r}", f"gk{r}"), writes=(f"gk{r}",))
                ctl.op("vector",
                       lambda e, r=r: e.reciprocal(g_rz[:, r:r + 1],
                                                   g_z[:, r:r + 1]),
                       reads=(f"gz{r}",), writes=(f"grz{r}",))
                ctl.op("vector",
                       lambda e, t=t, r=r: e.tensor_scalar(
                           gates[:, t, :], g_k[:, r, :], g_rz[:, r:r + 1], 0.5,
                           ALU.mult, ALU.mult),
                       reads=(f"gk{r}", f"grz{r}"), writes=(f"gates{t}",))

        def ffn(call):
            fctx = ExitStack()
            with fctx:
                fsb = lambda *a: fctx.enter_context(nc.sbuf_tensor(*a))
                ln_fm = fsb(f"ln_fm{call}", [128, c.DC, c.T], BF16)
                gates = fsb(f"gates{call}", [128, c.TT, c.E], F32)
                sq = fsb(f"sq{call}", [128, c.D], F32)
                lntm = fsb(f"lntm{call}", [128, 2, c.D], BF16)

                w1b = fsb(f"w1b{call}", [128, 2, c.JCG, c.DC, 128], BF16)
                w2b = fsb(f"w2b{call}", [128, 2, c.JCG, c.D], BF16)
                h_sb = fsb(f"h_sb{call}", [128, 3, c.TB], BF16)

                LK = f"lf{call}_"
                n_gran = c.E * c.NG

                NSUB = max(1, c.JCG // 2)
                def issue_gran(idx):
                    e_, g_ = divmod(idx, c.NG)
                    par = idx % 2
                    w = c.JCG // NSUB
                    for k in range(NSUB):
                        jl, jh = k * w, (k + 1) * w
                        ctl.dma("gpsimd", f"d_w{par}s{k}",
                                [(lambda en, e_=e_, g_=g_, par=par, jl=jl, jh=jh:
                                  en.dma_start(
                                    out=w1b[:, par, jl:jh, :, :].rearrange(
                                        "p a b k -> p (a b k)"),
                                    in_=w1s[e_, g_, :,
                                            jl * c.DC * 128:jh * c.DC * 128])),
                                 (lambda en, e_=e_, g_=g_, par=par, jl=jl, jh=jh:
                                  en.dma_start(
                                    out=w2b[:, par, jl:jh, :].rearrange(
                                        "p a k -> p (a k)"),
                                    in_=w2s[e_, g_, :, jl * c.D:jh * c.D]))],
                                writes=(f"w1b{par}s{k}", f"w2b{par}s{k}"))

                issue_gran(0)
                if n_gran > 1:
                    issue_gran(1)
                fast = list(range(c.TB // 128))
                rest = list(range(c.TB // 128, c.TT))
                if call == 0:
                    ln_stats(sq[:], f"sq{call}", fast)
                ln_and_fm(ln_fm, LK, lntm, fast)
                router(ln_fm, LK, gates, fast)
                if call == 0:
                    ln_stats(sq[:], f"sq{call}", rest)
                ln_and_fm(ln_fm, LK, lntm, rest)
                router(ln_fm, LK, gates, rest)

                for idx in range(n_gran):
                    e_, g_ = divmod(idx, c.NG)
                    par = idx % 2
                    for tb in range(c.TBN):
                        t0 = tb * c.TB
                        tiles = (tb * 2, tb * 2 + 1)

                        def h_group(jj, ph, par=par, t0=t0, tiles=tiles):
                            psh = (ps_c, ps_d)[ph]
                            sk = jj // (c.JCG // NSUB)
                            ctl.group("tensor",
                                      [(lambda e, dc=dc, jj=jj, psh=psh, par=par,
                                        t0=t0: e.matmul(
                                          psh[:, 0:c.TB],
                                          w1b[:, par, jj, dc, :],
                                          ln_fm[:, dc, t0:t0 + c.TB],
                                          start=(dc == 0), stop=(dc == c.DC - 1)))
                                       for dc in range(c.DC)],
                                      reads=(f"w1b{par}s{sk}",) + tuple(
                                          f"{LK}{t}" for t in tiles),
                                      writes=("pC" if ph == 0 else "pD",))

                        def gelu(jj, ph):
                            psh = (ps_c, ps_d)[ph]
                            hs = jj % 3
                            ctl.op("scalar",
                                   lambda e, hs=hs, psh=psh: e.activation(
                                       h_sb[:, hs, :], psh[:, 0:c.TB], AF.Gelu),
                                   reads=("pC" if ph == 0 else "pD",),
                                   writes=(f"hsb{hs}",))

                        def y_group(jj, ph, par=par):
                            hs = jj % 3

                            sk = jj // (c.JCG // NSUB)

                            def mk(tt, ic, b, jj=jj, hs=hs, par=par):
                                return lambda e: e.matmul(
                                    ps_a[:, b * 512:(b + 1) * 512],
                                    h_sb[:, hs, tt * 128:(tt + 1) * 128],
                                    w2b[:, par, jj, ic * 512:(ic + 1) * 512],
                                    start=(jj == 0), stop=(jj == c.JCG - 1))
                            if jj == 0:
                                for tt in range(2):
                                    for ic in range(2):
                                        b = 2 * tt + ic
                                        ctl.group("tensor", [mk(tt, ic, b)],
                                                  reads=(f"hsb{hs}",
                                                         f"w2b{par}s{sk}"),
                                                  writes=(f"pA{b}",))
                            else:
                                fns = [mk(tt, ic, 2 * tt + ic)
                                       for tt in range(2) for ic in range(2)]
                                ctl.group("tensor", fns,
                                          reads=(f"hsb{hs}", f"w2b{par}s{sk}"),
                                          writes=("pA0", "pA1", "pA2", "pA3"))

                        h_group(0, 0)
                        gelu(0, 0)
                        for jj in range(1, c.JCG):
                            ph = jj % 2
                            h_group(jj, ph)
                            gelu(jj, ph)
                            y_group(jj - 1, (jj - 1) % 2)
                        y_group(c.JCG - 1, (c.JCG - 1) % 2)

                        for tt in range(2):
                            t = tiles[tt]
                            for ic in range(2):
                                b = 2 * tt + ic
                                ctl.op("vector",
                                       lambda e, t=t, ic=ic, b=b, e_=e_:
                                       e.scalar_tensor_tensor(
                                           x_sb[:, t, ic * 512:(ic + 1) * 512],
                                           ps_a[:, b * 512:(b + 1) * 512],
                                           gates[:, t, e_:e_ + 1],
                                           x_sb[:, t, ic * 512:(ic + 1) * 512],
                                           ALU.mult, ALU.add),
                                       reads=(f"pA{b}", f"gates{t}", f"x{t}"),
                                       writes=(f"x{t}",))
                    if idx + 2 < n_gran:
                        issue_gran(idx + 2)
                ctl.barrier()
                ctl.flush()

        def attn():
            actx = ExitStack()
            with actx:
                asb = lambda *a: actx.enter_context(nc.sbuf_tensor(*a))
                q_fm = asb("q_fm", [128, c.D // 256, c.T], BF16)
                k_fm = asb("k_fm", [128, c.D // 256, c.T], BF16)
                v_tm = asb("v_tm", [128, c.TT, c.D], BF16)
                pctx = ExitStack()
                psb = lambda *a: pctx.enter_context(nc.sbuf_tensor(*a))
                ln2 = psb("ln2", [128, 2, c.DC, 512], BF16)
                sqa = psb("sqa", [128, c.D], F32)
                lntma = psb("lntma", [128, 2, c.D], BF16)
                wqk = psb("wqk", [128, c.DC, c.D], BF16)
                wv = psb("wv", [128, c.DC, c.D], BF16)

                ctl.dma("gpsimd", "d_ini",
                        [lambda e: e.dma_start(
                            out=wqk[:].rearrange("p dc k -> p (dc k)"), in_=wqks),
                         lambda e: e.dma_start(
                            out=wv[:].rearrange("p dc k -> p (dc k)"), in_=wvs)],
                        writes=("wqk", "wv"))

                for blk in range(c.NTB):
                    r = blk % 2
                    for tt in range(4):
                        t = blk * 4 + tt
                        ctl.op("scalar",
                               lambda e, t=t: e.activation(
                                   sqa[:], x_sb[:, t, :], AF.Square,
                                   accum_out=st_qq[:, t:t + 1]),
                               reads=(f"x{t}",), writes=("sqa", f"qq{t}"))
                        ctl.op("vector",
                               lambda e, t=t: e.reduce_sum(
                                   st_s[:, t:t + 1], x_sb[:, t, :], axis=AX.X),
                               reads=(f"x{t}",), writes=(f"s{t}",))
                        ctl.op("vector",
                               lambda e, t=t: e.tensor_scalar(
                                   st_nm[:, t:t + 1], st_s[:, t:t + 1],
                                   -1.0 / c.D, None, ALU.mult),
                               reads=(f"s{t}",), writes=(f"nm{t}",))
                        ctl.op("vector",
                               lambda e, t=t: e.tensor_mul(
                                   st_m2[:, t:t + 1], st_nm[:, t:t + 1],
                                   st_nm[:, t:t + 1]),
                               reads=(f"nm{t}",), writes=(f"m2{t}",))
                        ctl.op("vector",
                               lambda e, t=t: e.scalar_tensor_tensor(
                                   st_var[:, t:t + 1], st_qq[:, t:t + 1],
                                   1.0 / c.D, st_m2[:, t:t + 1],
                                   ALU.mult, ALU.subtract),
                               reads=(f"qq{t}", f"m2{t}"), writes=(f"var{t}",))
                        ctl.op("scalar",
                               lambda e, t=t: e.activation(
                                   st_std[:, t:t + 1], st_var[:, t:t + 1],
                                   AF.Sqrt, bias=epsc[:]),
                               reads=(f"var{t}", "eps"), writes=(f"std{t}",))
                        ctl.op("vector",
                               lambda e, t=t: e.reciprocal(
                                   st_rs[:, t:t + 1], st_std[:, t:t + 1]),
                               reads=(f"std{t}",), writes=(f"rs{t}",))
                        rr = t % 2
                        ctl.op("vector",
                               lambda e, t=t, rr=rr: e.tensor_scalar(
                                   lntma[:, rr, :], x_sb[:, t, :],
                                   st_nm[:, t:t + 1], st_rs[:, t:t + 1],
                                   ALU.add, ALU.mult),
                               reads=(f"x{t}", f"nm{t}", f"rs{t}"),
                               writes=(f"lntma{rr}",))
                        for hb, pf in ((0, ps_f0), (1, ps_f1)):
                            ctl.group("tensor",
                                      [(lambda e, rr=rr, dc=dc, hb=hb, pf=pf:
                                        e.matmul(
                                          pf[:, (dc - hb * 4) * 128:
                                             (dc - hb * 4 + 1) * 128],
                                          lntma[:, rr, dc * 128:(dc + 1) * 128],
                                          idn[:], start=True, stop=True))
                                       for dc in range(hb * 4, hb * 4 + 4)],
                                      reads=(f"lntma{rr}", "idn"),
                                      writes=(f"pf{hb}",))
                            ctl.op("scalar",
                                   lambda e, r=r, tt=tt, hb=hb, pf=pf: e.copy(
                                       ln2[:, r, hb * 4:hb * 4 + 4,
                                           tt * 128:(tt + 1) * 128],
                                       pf[:].rearrange("p (dc k) -> p dc k",
                                                       k=128)),
                                   reads=(f"pf{hb}",), writes=(f"ln2_{r}_{tt}",))
                    lnreads = tuple(f"ln2_{r}_{tt}" for tt in range(4))
                    for uc in range(c.D // 256):
                        for half, (dst, dkey) in enumerate(
                                ((q_fm, "qfm"), (k_fm, "kfm"))):
                            ctl.group("tensor",
                                      [(lambda e, uc=uc, half=half, dc=dc, r=r:
                                        e.matmul(
                                          ps_c[:],
                                          wqk[:, dc,
                                              half * 512 + uc * 128:
                                              half * 512 + (uc + 1) * 128],
                                          ln2[:, r, dc, :],
                                          start=(dc == 0), stop=(dc == c.DC - 1)))
                                       for dc in range(c.DC)],
                                      reads=lnreads + ("wqk",), writes=("pC",))
                            if half == 0:
                                ctl.op("vector",
                                       lambda e, uc=uc, blk=blk, dst=dst:
                                       e.tensor_scalar(
                                           dst[:, uc, blk * 512:(blk + 1) * 512],
                                           ps_c[:], 0.125, None, ALU.mult),
                                       reads=("pC",), writes=(dkey,))
                            else:
                                ctl.op("scalar",
                                       lambda e, uc=uc, blk=blk, dst=dst: e.copy(
                                           dst[:, uc, blk * 512:(blk + 1) * 512],
                                           ps_c[:]),
                                       reads=("pC",), writes=(dkey,))
                    for tt in range(4):
                        t = blk * 4 + tt
                        for ic in range(2):
                            ctl.group("tensor",
                                      [(lambda e, tt=tt, ic=ic, dc=dc, r=r:
                                        e.matmul(
                                          ps_d[:],
                                          ln2[:, r, dc, tt * 128:(tt + 1) * 128],
                                          wv[:, dc, ic * 512:(ic + 1) * 512],
                                          start=(dc == 0), stop=(dc == c.DC - 1)))
                                       for dc in range(c.DC)],
                                      reads=lnreads + ("wv",), writes=("pD",))
                            ctl.op("scalar",
                                   lambda e, t=t, ic=ic: e.copy(
                                       v_tm[:, t, ic * 512:(ic + 1) * 512],
                                       ps_d[:]),
                                   reads=("pD",), writes=("vtm",))

                ctl.barrier()
                pctx.close()
                wo = asb("wo", [128, c.DC, c.D], BF16)
                bbc = asb("bbc", [128, c.D], BF16)
                strip = asb("strip", [128, 2, STRW], BF16)
                pbuf = asb("pbuf", [128, 2, c.T], BF16)
                sc_sb = asb("sc_sb", [128, 2, c.T], F32)
                dgz = asb("dgz", [128, 2, 128], BF16)
                otmp2 = asb("otmp2", [128, 512], BF16)
                p_t = asb("p_t", [128, c.TT, 512], BF16)
                aout = asb("aout", [128, c.NH, 512], BF16)
                otmp = asb("otmp", [128, 512], BF16)
                ctl.dma("gpsimd", "d_ini",
                        [lambda e: e.dma_start(
                            out=wo[:].rearrange("p dc k -> p (dc k)"), in_=wos)],
                        writes=("wo",))
                ctl.dma("gpsimd", "d_bb",
                        [lambda e: e.dma_start(
                            out=bbc[:],
                            in_=b_out_t[None, :].broadcast_to([128, c.D]))],
                        writes=("bbc",))

                def emit_S(j, t1b, h, tt, half):
                    uc, po = h // 2, (h % 2) * 64
                    t1 = t1b * 4 + tt
                    nk = c.T // 512
                    kbs = tuple(range((nk + 1) // 2)) if half == 0 else tuple(
                        range((nk + 1) // 2, nk))
                    if not kbs:
                        return
                    ctl.group("tensor",
                              [(lambda e, kb=kb, uc=uc, po=po, t1=t1: e.matmul(
                                  ps_a[:, kb * 512:(kb + 1) * 512],
                                  q_fm[po:po + 64, uc, t1 * 128:(t1 + 1) * 128],
                                  k_fm[po:po + 64, uc, kb * 512:(kb + 1) * 512],
                                  start=True, stop=True))
                               for kb in kbs],
                              reads=("qfm", "kfm"),
                              writes=tuple(f"pA{kb}" for kb in kbs))

                def emit_B(j, t1b, h, tt, half):
                    r, spar = j % 2, (j // 4) % 2
                    so = 384 - 128 * tt
                    nk = c.T // 512
                    k0 = (nk + 1) // 2
                    kbs = tuple(range(k0)) if half == 0 else tuple(range(k0, nk))
                    if not kbs:
                        return
                    lo = 0 if half == 0 else k0 * 512
                    HW = (k0 * 512 - lo) if half == 0 else (c.T - lo)
                    ctl.op("vector",
                           lambda e, r=r, spar=spar, so=so, lo=lo, HW=HW:
                           e.tensor_add(
                               sc_sb[:, r, lo:lo + HW], ps_a[:, lo:lo + HW],
                               strip[:, spar, so + lo:so + lo + HW]),
                           reads=(f"strip{spar}",) + tuple(
                               f"pA{kb}" for kb in kbs),
                           writes=(f"scb{r}h{half}",))

                def emit_chc(j, h):
                    r = j % 2
                    ctl.op("vector",
                           lambda e, r=r: e.reduce_max(
                               negb[:], sc_sb[:, r, :], axis=AX.X, negate=True),
                           reads=(f"scb{r}h0", f"scb{r}h1"), writes=("smaxh",))
                    ctl.op("tensor",
                           lambda e: e.matmul(ps_f0[0:1, 0:128], negb[:], idn[:],
                                              start=True, stop=True),
                           reads=("smaxh", "idn"), writes=("pf0",))
                    ctl.op("vector",
                           lambda e: e.tensor_reduce(
                               sc1t[:], ps_f0[0:1, 0:128], axis=AX.X, op=ALU.min),
                           reads=("pf0",), writes=("sc1t",))
                    ctl.op("tensor",
                           lambda e: e.matmul(ps_f0[:, 128:129], onesf[:],
                                              sc1t[:], start=True, stop=True),
                           reads=("onesf", "sc1t"), writes=("pf0",))
                    ctl.op("vector",
                           lambda e, h=h: e.tensor_scalar(
                               chC[:, h:h + 1], ps_f0[:, 128:129], -4.0, None,
                               ALU.add),
                           reads=("pf0",), writes=(f"chC{h}",))

                def emit_E(j, h):
                    r = j % 2
                    ctl.op("scalar",
                           lambda e, r=r, h=h: e.activation(
                               pbuf[:, r, :], sc_sb[:, r, :], AF.Exp,
                               bias=chC[:, h:h + 1], accum_out=g_z[:, r:r + 1]),
                           reads=(f"scb{r}h0", f"scb{r}h1", f"chC{h}"),
                           writes=(f"pbuf{r}", f"sz{r}"))

                def emit_norm(j):
                    r = j % 2
                    ctl.op("vector",
                           lambda e, r=r: e.reciprocal(g_rz[:, r:r + 1],
                                                       g_z[:, r:r + 1]),
                           reads=(f"sz{r}",), writes=(f"srz{r}",))
                    ctl.op("vector",
                           lambda e, r=r: e.tensor_scalar(
                               dgz[:, r, :], idn[:], g_rz[:, r:r + 1],
                               None, ALU.mult),
                           reads=("idn", f"srz{r}",), writes=(f"dgz{r}",))

                def emit_T(j, tt):
                    r = j % 2
                    for w in range(c.TT // 4):
                        pf = (ps_f0, ps_f1)[w % 2]
                        ctl.group("tensor",
                                  [(lambda e, r=r, w=w, q=q, pf=pf: e.matmul(
                                      pf[:, q * 128:(q + 1) * 128],
                                      pbuf[:, r, (4 * w + q) * 128:
                                           (4 * w + q + 1) * 128],
                                      dgz[:, r, :], start=True, stop=True))
                                   for q in range(4)],
                                  reads=(f"pbuf{r}", f"dgz{r}"),
                                  writes=(f"pf{w % 2}",))
                        eng = "scalar" if w % 2 == 0 else "vector"
                        if eng == "scalar":
                            fn = (lambda e, w=w, tt=tt, pf=pf: e.copy(
                                p_t[:, 4 * w:4 * w + 4, tt * 128:(tt + 1) * 128],
                                pf[:].rearrange("p (a b) -> p a b", b=128)))
                        else:
                            fn = (lambda e, w=w, tt=tt, pf=pf: e.tensor_copy(
                                p_t[:, 4 * w:4 * w + 4, tt * 128:(tt + 1) * 128],
                                pf[:].rearrange("p (a b) -> p a b", b=128)))
                        ctl.op(eng, fn, reads=(f"pf{w % 2}",), writes=("ptb",))

                def emit_av(h):
                    ctl.group("tensor",
                              [(lambda e, t2c=t2c, h=h: e.matmul(
                                  ps_c[:], v_tm[:, t2c, h * 128:(h + 1) * 128],
                                  p_t[:, t2c, :],
                                  start=(t2c == 0), stop=(t2c == c.TT - 1)))
                               for t2c in range(c.TT)],
                              reads=("ptb", "vtm"), writes=("pC",))
                    ctl.op("scalar",
                           lambda e, h=h: e.copy(aout[:, h, :], ps_c[:]),
                           reads=("pC",), writes=(f"aout{h}",))

                def emit_outproj(t1b):
                    areads = tuple(f"aout{h}" for h in range(c.NH))
                    for tt in range(4):
                        t = t1b * 4 + tt
                        for ic in range(2):
                            par = (2 * tt + ic) % 2
                            pso = (ps_d, ps_c)[par]
                            pkey = ("pD", "pC")[par]
                            ot = (otmp, otmp2)[par]
                            okey = ("otmp", "otmp2")[par]
                            ctl.group("tensor",
                                      [(lambda e, hh=hh, tt=tt, ic=ic, pso=pso:
                                        e.matmul(
                                          pso[:],
                                          aout[:, hh, tt * 128:(tt + 1) * 128],
                                          wo[:, hh, ic * 512:(ic + 1) * 512],
                                          start=(hh == 0), stop=(hh == c.NH - 1)))
                                       for hh in range(c.NH)],
                                      reads=areads + ("wo",), writes=(pkey,))
                            ctl.op("vector",
                                   lambda e, ic=ic, pso=pso, ot=ot: e.tensor_add(
                                       ot[:], pso[:],
                                       bbc[:, ic * 512:(ic + 1) * 512]),
                                   reads=(pkey, "bbc"), writes=(okey,))
                            ctl.op("vector",
                                   lambda e, t=t, ic=ic, ot=ot: e.tensor_add(
                                       x_sb[:, t, ic * 512:(ic + 1) * 512],
                                       x_sb[:, t, ic * 512:(ic + 1) * 512],
                                       ot[:]),
                                   reads=(okey, f"x{t}"), writes=(f"x{t}",))

                jobs = [(t1b, h, tt) for t1b in range(c.NTB)
                        for h in range(c.NH) for tt in range(4)]
                pending = []

                def drain_one():
                    pj, pt1b, ph, ptt = pending.pop(0)
                    emit_T(pj, ptt)
                    if ptt == 3:
                        emit_av(ph)
                        if ph == c.NH - 1:
                            emit_outproj(pt1b)

                prev = None
                for j, (t1b, h, tt) in enumerate(jobs):
                    if tt == 0:
                        spar = (j // 4) % 2
                        off = (c.NTB - 1 - t1b) * 512
                        ctl.dma("gpsimd", f"d_s{spar}",
                                [(lambda e, h=h, off=off, spar=spar: e.dma_start(
                                    out=strip[:, spar, :],
                                    in_=strips[h, :, off:off + STRW]))],
                                writes=(f"strip{spar}",))
                    emit_S(j, t1b, h, tt, 0)
                    emit_B(j, t1b, h, tt, 0)
                    emit_S(j, t1b, h, tt, 1)
                    if prev is not None:
                        emit_norm(prev)
                    emit_B(j, t1b, h, tt, 1)
                    if t1b == 0 and tt == 0:
                        emit_chc(j, h)
                    if len(pending) >= 2:
                        drain_one()
                    emit_E(j, h)
                    pending.append((j, t1b, h, tt))
                    prev = j
                emit_norm(prev)
                while pending:
                    drain_one()
                sqv = sc_sb[:].rearrange("p a b -> p (a b)")[:, 0:c.D]
                ln_stats(sqv, "scb0h0")
                ctl.barrier()
                ctl.flush()

        def dump_x(dst):
            d3 = dst.rearrange("(t p) d -> t p d", p=128)
            ctl.dma("sync", "d_dbg",
                    [(lambda e, t=t: e.dma_start(out=d3[t], in_=x_sb[:, t, :]))
                     for t in range(c.TT)],
                    reads=tuple(f"x{t}" for t in range(c.TT)))
            ctl.barrier()

        ffn(0)
        if dbg:
            dump_x(dbg_x1)
        attn()
        if dbg:
            dump_x(dbg_x2)
            ctl.dma("sync", "d_dbg",
                    [lambda e: e.dma_start(out=dbg_chc, in_=chC[:])],
                    reads=tuple(f"chC{h}" for h in range(c.NH)))
            ctl.barrier()
        ffn(1)

        y3 = y_out.rearrange("(t p) d -> t p d", p=128)
        ctl.dma("sync", "d_out",
                [(lambda e, t=t: e.dma_start(out=y3[t], in_=x_sb[:, t, :]))
                 for t in range(c.TT)],
                reads=tuple(f"x{t}" for t in range(c.TT)))
        ctl.barrier()
        ctl.flush()
    return nc


def _prep_host(cfg, W_qk, W_v, W_out, b_out, conv, router_w, w1, w2):
    c = cfg
    JGW = c.JCG * 128

    def rows_tiled(m):
        D, K = m.shape
        return np.ascontiguousarray(
            m.reshape(D // 128, 128, K).transpose(1, 0, 2).reshape(128, -1))

    w1s = np.empty((c.E, c.NG, 128, c.JCG * c.DC * 128), dtype=np.float32)
    w2s = np.empty((c.E, c.NG, 128, c.JCG * c.D), dtype=np.float32)
    for e in range(c.E):
        w1t = np.ascontiguousarray(w1[e].T)
        w2t = np.ascontiguousarray(w2[e].T)
        for g in range(c.NG):
            blk = w1t[:, g * JGW:(g + 1) * JGW]
            b4 = blk.reshape(c.DC, 128, c.JCG, 128).transpose(1, 2, 0, 3)
            w1s[e, g] = np.ascontiguousarray(b4).reshape(128, -1)
            blk2 = w2t[g * JGW:(g + 1) * JGW, :]
            w2s[e, g] = rows_tiled(blk2)

    wqks = rows_tiled(np.ascontiguousarray(W_qk))
    wvs = rows_tiled(np.ascontiguousarray(W_v))
    wos = rows_tiled(np.ascontiguousarray(W_out))
    wrs = rows_tiled(np.ascontiguousarray(router_w.T))

    SFULL = c.T + 1920
    base = c.T - 128
    cols = np.arange(SFULL)
    p = np.arange(128)
    d = cols[None, :] - base - p[:, None]
    idx = np.clip(d, -c.MAXPOS, c.MAXPOS) + c.MAXPOS
    strips = np.ascontiguousarray(conv[:, idx]).astype(np.float32)

    return dict(w1s=w1s, w2s=w2s, wqks=wqks, wvs=wvs, wos=wos, wrs=wrs,
                b_out_t=np.ascontiguousarray(b_out).astype(np.float32),
                strips=strips)


def kernel(L0_attn_x, L0_attn_mask, L0_attn_W_qk, L0_attn_W_v, L0_attn_W_out,
           L0_attn_b_out, L0_attn_conv, L0_ffn_router_w, L0_ffn_w1, L0_ffn_w2):
    cfg = CFG()
    x = np.ascontiguousarray(np.asarray(L0_attn_x, dtype=np.float32))
    B = x.shape[0]
    shared = _prep_host(cfg,
                        np.asarray(L0_attn_W_qk, np.float32),
                        np.asarray(L0_attn_W_v, np.float32),
                        np.asarray(L0_attn_W_out, np.float32),
                        np.asarray(L0_attn_b_out, np.float32),
                        np.asarray(L0_attn_conv, np.float32),
                        np.asarray(L0_ffn_router_w, np.float32),
                        np.asarray(L0_ffn_w1, np.float32),
                        np.asarray(L0_ffn_w2, np.float32))
    nc = build_program(cfg)
    in_maps = [dict(shared, x_in=np.ascontiguousarray(x[b])) for b in range(B)]
    res = run_bass_kernel_spmd(nc, in_maps, core_ids=list(range(B)))
    out = np.stack([res.results[b]["y_out"] for b in range(B)], axis=0)
    return out
